# version 14
# baseline (speedup 1.0000x reference)
from contextlib import ExitStack

import numpy as np
import concourse.bass as bass
import concourse.mybir as mybir
from concourse.bass_utils import run_bass_kernel_spmd

F32 = mybir.dt.float32
FP16 = mybir.dt.float16
AF = mybir.ActivationFunctionType
ALU = mybir.AluOpType

DMODEL = 4096
DH = 2048
NDT = DH // 128
NST = 16
BNECK = 512
NEXP = 64
TC = 512
NTT = TC // 128
NGRP = 4
NGPT = NST // NGRP
SEGL = TC + 1
LN_EPS = 1e-5
HEAVY_SWEEPS = False
HEAVY_TAIL = False
SERIAL_BACK = False
PIN_TABLE = True
DRAIN_PER_DTILE = 3
RG = [[0, 1], [2, 3], [4, 5], [6, 7]]


def build_program(seq_len, debug=False):
    nch = seq_len // TC
    nc = bass.Bass("TRN2", target_bir_lowering=False, num_devices=8,
                   detect_race_conditions=False)

    def din(name, shape, dt=F32):
        return nc.dram_tensor(name, shape, dt, kind="ExternalInput").ap()

    x_in = din("x_in", [seq_len, DH])
    awt = din("awt", [DH, BNECK])
    bwt = din("bwt", [BNECK, DH])
    wqt = din("wqt", [DH, 16])
    keyst = din("keyst", [16, NEXP])
    xpt = din("xpt", [DH, 160])
    dtwt = din("dtwt", [128, DH])
    dtb = din("dtb", [128, NDT])
    alog = din("alog", [128, NDT * NST])
    dvec = din("dvec", [128, NDT])
    ident16 = din("ident16", [128, 128], FP16)
    ident32 = din("ident32", [128, 128], F32)
    sel = din("sel", [NEXP, 4 * 128], FP16)

    out = nc.dram_tensor("out", [seq_len, DH], F32, kind="ExternalOutput").ap()
    dbg = {}
    if debug:
        for nm, shape in [("dbg_x1", [seq_len, DH]), ("dbg_x2", [seq_len, DH]),
                          ("dbg_yt", [128, NTT * DH])]:
            dbg[nm] = nc.dram_tensor(nm, shape, F32, kind="ExternalOutput").ap()

    bnc = {}
    for ch in range(nch):
        for tag, shape, dt in [("s1", [128, 2 * NTT], F32),
                               ("s2", [128, 2 * NTT], F32),
                               ("s3", [128, 2 * NTT], F32),
                               ("zq1", [132, 4 * TC], FP16),
                               ("zq2", [132, 4 * TC], FP16),
                               ("xp", [160, TC], F32)]:
            bnc[(ch, tag, "i")] = nc.dram_tensor(f"b_{tag}_{ch}_i", shape, dt)
            bnc[(ch, tag, "o")] = nc.dram_tensor(f"b_{tag}_{ch}_o", shape, dt)

    ctx = ExitStack()
    sb = lambda n, s, d: ctx.enter_context(nc.sbuf_tensor(n, s, d))
    psm = lambda n, s, d: ctx.enter_context(nc.psum_tensor(n, s, d))

    w_awt = sb("w_awt", [128, NDT * BNECK], FP16)
    w_bwt = sb("w_bwt", [128, 4 * DH], FP16)
    w_wqt = sb("w_wqt", [128, NDT * 16], FP16)
    w_keys = sb("w_keys", [16, NEXP], FP16)
    w_xpt = sb("w_xpt", [128, NDT * 160], FP16)
    w_dtwt = sb("w_dtwt", [128, DH], FP16)
    w_dtbn = sb("w_dtbn", [128, NDT], F32)
    w_absA = sb("w_absA", [128, NDT * NST], F32)
    w_D = sb("w_D", [128, NDT], F32)
    w_id16 = sb("w_id16", [128, 128], FP16)
    w_sel = sb("w_sel", [64, 4 * 128], FP16)

    x1_sb = [sb("x1_a", [128, NTT * DH], FP16),
             sb("x1_b", [128, NTT * DH], FP16)]
    xh_sb = [sb("xh_a", [128, NTT * DH], FP16),
             sb("xh_b", [128, NTT * DH], FP16)]

    xhT = sb("xhT", [128, 2 * TC], FP16)
    stats = sb("stats", [128, 2 * NTT], F32)
    stats_r = sb("stats_r", [128, 2 * NTT], F32)
    mvec = sb("mvec", [128, 3 * 8 * NTT], F32)
    gvec = sb("gvec", [128, 2 * 8 * NTT], F32)
    zq_sb = sb("zq_sb", [128, 5 * TC], FP16)
    zq_r = sb("zq_r", [128, 4 * TC], FP16)
    q16 = sb("q16", [16, TC], FP16)
    sc_g = sb("sc_g", [128, 2 * NTT * NEXP], F32)
    cmp_g = sb("cmp_g", [128, NEXP * 16], FP16)
    rk_g = sb("rk_g", [128, NEXP], F32)
    rk_t = sb("rk_t", [128, NEXP], F32)
    mt_g = sb("mt_g", [128, 2], F32)
    msk16 = sb("msk16", [128, 2 * NTT * NEXP], F32)
    gex = sb("gex", [128, NTT * NEXP], F32)
    gate = sb("gate", [128, NTT * NEXP], FP16)
    gateT = sb("gateT", [64, TC], FP16)
    grep = sb("grep", [128, 4 * TC], FP16)
    xp_d = sb("xp_d", [128, TC], F32)
    xp_bc = sb("xp_bc", [32, TC], F32)
    dtin = sb("dtin", [128, TC], FP16)
    b_rep = sb("b_rep", [128, NST * TC], FP16)
    c_rep = sb("c_rep", [128, NST * TC], FP16)
    tsg = sb("tsg", [128, TC], F32)
    L_d = sb("L_d", [128, 2 * TC], FP16)
    u_d = sb("u_d", [128, TC], FP16)
    x2T_d = sb("x2T_d", [128, 2 * TC], FP16)
    dAb = sb("dAb", [128, 4 * NGRP * SEGL], FP16)
    dBxb = sb("dBxb", [128, NGRP * SEGL], FP16)
    hb = sb("hb", [128, NGRP * SEGL], FP16)
    yacc = sb("yacc", [128, TC], FP16)
    carry = sb("carry", [128, NDT * NST], FP16)
    zc = sb("zc", [128, 1], FP16)
    ytot = sb("ytot", [128, TC], FP16)

    ps_z = psm("ps_z", [128, 2048], F32)
    ps_a = psm("ps_a", [128, 512], F32)
    ps_b = psm("ps_b", [128, 512], F32)
    ps_tr = psm("ps_tr", [128, 1024], FP16)
    ps_yt = psm("ps_yt", [128, 1024], FP16)

    sems = {n: ctx.enter_context(nc.semaphore(n))
            for n in ["sd", "gd", "cc", "sv", "ss", "st", "sg"]}
    cnt = {n: 0 for n in sems}

    blk = ctx.enter_context(nc.Block())

    def on(eng, fn):
        getattr(blk, eng)(fn)

    def bump(name, k=1):
        cnt[name] += k
        return cnt[name]

    def dma(q, out_ap, in_ap, s="sd"):
        q.dma_start(out=out_ap, in_=in_ap).then_inc(sems[s], 16)
        return bump(s, 16)

    trk = {
        "pstr": [("ss", 0), ("ss", 0)],
        "xhT": [0, 0],
        "psyt": ("ss", 0),
        "psytg": ("ss", 0),
        "psz_rd": 0,
        "psa": 0,
        "zq_gd": 0,
        "stats_gd": 0,
        "statsr_rd": 0,
        "zgT_rd": 0,
        "grep_rd": 0,
        "qpsb": ("ss", 0),
        "xh_rd": [0, 0],
        "xh_rd_v": [0, 0],
        "x1_free": [None, None],
        "Lx2_rd": [0, 0],
        "ytot_rd": 0,
        "dA_rd": None,
        "scg": [0, 0],
        "gex_rd": 0,
        "gate_rd": 0,
        "gateT_rd": 0,
        "zqr_rd": 0,
        "q16_rd": 0,
    }

    def _(g):
        for kd in range(NDT):
            dma(g, w_awt[:, kd * BNECK:(kd + 1) * BNECK],
                awt[kd * 128:(kd + 1) * 128, :], "gd")
        for kb in range(4):
            dma(g, w_bwt[:, kb * DH:(kb + 1) * DH],
                bwt[kb * 128:(kb + 1) * 128, :], "gd")
        for kd in range(NDT):
            dma(g, w_wqt[:, kd * 16:(kd + 1) * 16],
                wqt[kd * 128:(kd + 1) * 128, :], "gd")
        dma(g, w_keys[:, :], keyst, "gd")
        for kd in range(NDT):
            dma(g, w_xpt[:, kd * 160:(kd + 1) * 160],
                xpt[kd * 128:(kd + 1) * 128, :], "gd")
        dma(g, w_dtwt[:, :], dtwt, "gd")
        dma(g, w_id16[:, :], ident16, "gd")
        dma(g, w_sel[:, :], sel, "gd")
        dma(g, w_dtbn[:, :], dtb, "gd")
        dma(g, w_absA[:, :], alog, "gd")
        dma(g, w_D[:, :], dvec, "gd")
    on("gpsimd", _)
    wmark = cnt["gd"]

    def _(v):
        v.wait_ge(sems["gd"], wmark)
        v.memset(carry[:, :], 0.0)
        v.memset(zc[:, :], 0.0)
        v.memset(dAb[:, :].rearrange("p (k t) -> p k t", t=SEGL)[:, :, 0:1],
                 0.0).then_inc(sems["sv"], 1)
    on("vector", _)
    prol_v = bump("sv")

    def _(s):
        if PIN_TABLE:
            s.add_instruction(mybir.InstLoadActFuncSet(
                name=nc.get_next_instruction_name(), act_func_set_id=6,
                ins=[], outs=[]))
        s.wait_ge(sems["gd"], wmark)
        s.activation(out=w_absA[:, :], in_=w_absA[:, :],
                     func=AF.Exp).then_inc(sems["ss"], 1)
        s.wait_ge(sems["ss"], bump("ss"))
        s.activation(out=w_absA[:, :], in_=w_absA[:, :], func=AF.Identity,
                     scale=-1.0).then_inc(sems["ss"], 1)
    on("scalar", _)
    prol_s = bump("ss")

    def _(te):
        te.wait_ge(sems["gd"], wmark)
    on("tensor", _)

    taskq = []

    def defer(fn, heavy=False):
        fn._heavy = heavy
        taskq.append(fn)

    def drain(k=None, light_only=False):
        n = len(taskq) if k is None else min(k, len(taskq))
        for _ in range(n):
            if light_only and getattr(taskq[0], "_heavy", False):
                return
            taskq.pop(0)()

    def ln_phase(ch, tag, xbuf, dst, dep, plane):
        mk = {}
        x1 = x1_sb[xbuf]
        xh = xh_sb[dst]
        lnv = mvec[:, plane * 8 * NTT:(plane + 1) * 8 * NTT]

        def t1():
            gd_guard = trk["stats_gd"]
            xh_guard = trk["xh_rd"][dst]
            xh_guard_v = trk["xh_rd_v"][dst]

            def _(v):
                v.wait_ge(sems["sv"], dep["sv"])
                if dep.get("gd"):
                    v.wait_ge(sems["gd"], dep["gd"])
                v.wait_ge(sems["gd"], gd_guard)
                for tt in range(NTT):
                    ins = v.tensor_reduce(out=stats[:, 2 * tt:2 * tt + 1],
                                          in_=x1[:, tt * DH:(tt + 1) * DH],
                                          axis=mybir.AxisListType.X, op=ALU.add)
                ins.then_inc(sems["sv"], 1)
                mk["v"] = bump("sv")
            on("vector", _)

            def _(s):
                s.wait_ge(sems["sv"], dep["sv"])
                if dep.get("gd"):
                    s.wait_ge(sems["gd"], dep["gd"])
                s.wait_ge(sems["st"], xh_guard)
                s.wait_ge(sems["sv"], xh_guard_v)
                s.wait_ge(sems["ss"], prol_s)
                for tt in range(NTT):
                    ins = s.activation(out=xh[:, tt * DH:(tt + 1) * DH],
                                       in_=x1[:, tt * DH:(tt + 1) * DH],
                                       func=AF.Square)
                ins.then_inc(sems["ss"], 1)
                mk["s"] = bump("ss")
            on("scalar", _)

            def _(v):
                v.wait_ge(sems["ss"], mk["s"])
                for tt in range(NTT):
                    ins = v.tensor_reduce(out=stats[:, 2 * tt + 1:2 * tt + 2],
                                          in_=xh[:, tt * DH:(tt + 1) * DH],
                                          axis=mybir.AxisListType.X, op=ALU.add)
                ins.then_inc(sems["sv"], 1)
                mk["s2v"] = bump("sv")
            on("vector", _)
        defer(t1)

        def t2():
            sr_guard = trk["statsr_rd"]
            ccm = {}

            def _(g):
                g.wait_ge(sems["sv"], mk["v"])
                g.wait_ge(sems["sv"], mk["s2v"])
                dma(g, bnc[(ch, tag, "i")].ap(), stats[:, :], "gd")
                trk["stats_gd"] = cnt["gd"]
                g.wait_ge(sems["gd"], cnt["gd"])
                g.collective_compute(
                    "AllReduce", ALU.add, replica_groups=RG,
                    ins=[bnc[(ch, tag, "i")].ap()],
                    outs=[bnc[(ch, tag, "o")].ap()],
                ).then_inc(sems["cc"], 1)
                ccm["cc"] = bump("cc")
            on("gpsimd", _)

            def _(sy):
                sy.wait_ge(sems["cc"], ccm["cc"])
                sy.wait_ge(sems["sv"], sr_guard)
                dma(sy, stats_r[:, :], bnc[(ch, tag, "o")].ap(), "sd")
                mk["gd"] = cnt["sd"]
            on("sync", _)
        defer(t2)

        def t3():
            def _(v):
                v.wait_ge(sems["sd"], mk["gd"])
                mm = lnv.rearrange("p (t k) -> p t k", k=8)
                m_, v_ = mm[:, :, 0:1], mm[:, :, 1:2]
                sq_ = mm[:, :, 4:5]
                sr = stats_r[:, :].rearrange("p (t k) -> p t k", k=2)
                v.tensor_scalar_mul(m_, sr[:, :, 0:1], 1.0 / DMODEL)
                v.tensor_scalar_mul(v_, sr[:, :, 1:2], 1.0 / DMODEL).then_inc(
                    sems["sv"], 1)
                v.wait_ge(sems["sv"], bump("sv"))
                trk["statsr_rd"] = cnt["sv"]
                v.tensor_mul(sq_, m_, m_).then_inc(sems["sv"], 1)
                v.wait_ge(sems["sv"], bump("sv"))
                v.tensor_sub(v_, v_, sq_).then_inc(sems["sv"], 1)
                v.wait_ge(sems["sv"], bump("sv"))
                v.tensor_scalar_add(v_, v_, LN_EPS).then_inc(sems["sv"], 1)
                mk["v2"] = bump("sv")
            on("vector", _)

            def _(s):
                s.wait_ge(sems["sv"], mk["v2"])
                mm = lnv.rearrange("p (t k) -> p t k", k=8)
                s.activation(out=mm[:, :, 6:7], in_=mm[:, :, 1:2],
                             func=AF.Ln).then_inc(sems["ss"], 1)
                s.wait_ge(sems["ss"], bump("ss"))
                s.activation(out=mm[:, :, 2:3], in_=mm[:, :, 6:7],
                             func=AF.Exp, scale=-0.5).then_inc(sems["ss"], 1)
                mk["s2"] = bump("ss")
            on("scalar", _)

            def _(v):
                v.wait_ge(sems["ss"], mk["s2"])
                mm = lnv.rearrange("p (t k) -> p t k", k=8)
                m_, i_, nmi = mm[:, :, 0:1], mm[:, :, 2:3], mm[:, :, 3:4]
                v.scalar_tensor_tensor(out=nmi, in0=m_, scalar=-1.0, in1=i_,
                                       op0=ALU.mult,
                                       op1=ALU.mult).then_inc(sems["sv"], 1)
                mk["v3"] = bump("sv")
            on("vector", _)

            def _(v):
                v.wait_ge(sems["sv"], mk["v3"])
                for tt in range(NTT):
                    ins = v.tensor_scalar(out=xh[:, tt * DH:(tt + 1) * DH],
                                          in0=x1[:, tt * DH:(tt + 1) * DH],
                                          scalar1=lnv[:, 8 * tt + 2:8 * tt + 3],
                                          scalar2=lnv[:, 8 * tt + 3:8 * tt + 4],
                                          op0=ALU.mult, op1=ALU.add)
                ins.then_inc(sems["sv"], 1)
                mk["nmv"] = bump("sv")
            on("vector", _)

            def _(s):
                s.wait_ge(sems["sv"], mk["nmv"])
                s.sem_inc(sems["ss"], 1)
                mk["nm"] = bump("ss")
            on("scalar", _)
        defer(t3)
        return mk

    def sweep_phase(src_buf, lnmk, mm_emit, n_group=4, tail_cb=None):
        smk = {"t": {}, "e": {}, "m": {}}
        src = xh_sb[src_buf]
        for g0 in range(0, NDT, n_group):
            def tk(g0=g0):
                for kd in range(g0, g0 + n_group):
                    half = kd % 2

                    def _(te, kd=kd, half=half, pe=trk["pstr"][half]):
                        if kd == 0:
                            te.wait_ge(sems["ss"], lnmk["nm"])
                        te.wait_ge(sems[pe[0]], pe[1])
                        for tt in range(NTT):
                            ins = te.transpose(
                                out=ps_tr[:, half * 512 + tt * 128:
                                          half * 512 + (tt + 1) * 128],
                                in_=src[:, tt * DH + kd * 128:
                                        tt * DH + (kd + 1) * 128],
                                identity=w_id16[:, :])
                        ins.then_inc(sems["st"], 1)
                        smk["t"][kd] = bump("st")
                        trk["xh_rd"][src_buf] = smk["t"][kd]
                    on("tensor", _)

                    def _(v, kd=kd, half=half, xr=trk["xhT"][half]):
                        v.wait_ge(sems["st"], smk["t"][kd])
                        v.wait_ge(sems["st"], xr)
                        v.tensor_copy(xhT[:, half * TC:(half + 1) * TC],
                                      ps_tr[:, half * 512:
                                            (half + 1) * 512]).then_inc(
                                          sems["sv"], 1)
                        smk["e"][kd] = bump("sv")
                        trk["pstr"][half] = ("sv", smk["e"][kd])
                    on("vector", _)

                    def _(te, kd=kd, half=half):
                        te.wait_ge(sems["sv"], smk["e"][kd])
                        ins = mm_emit(te, kd, half)
                        ins.then_inc(sems["st"], 1)
                        smk["m"][kd] = bump("st")
                        trk["xhT"][half] = smk["m"][kd]
                    on("tensor", _)
                if tail_cb is not None and g0 + n_group >= NDT:
                    tail_cb()
            defer(tk, heavy=HEAVY_TAIL)
        return smk

    def ffn_phase(ch, src_buf, lnmk, tag, mbuf):
        mk = {}
        sc0 = mbuf * NTT * NEXP
        gb = mbuf * 8 * NTT

        def zmm(te, kd, half):
            if kd == 0:
                te.wait_ge(sems["sv"], trk["psz_rd"])
                q = trk["qpsb"]
                te.wait_ge(sems[q[0]], q[1])
            stop = (kd % 4 == 3)
            for mm in range(4):
                te.matmul(out=ps_z[:, mm * 512:(mm + 1) * 512],
                          lhsT=w_awt[:, kd * BNECK + mm * 128:
                                     kd * BNECK + (mm + 1) * 128],
                          rhs=xhT[:, half * TC:(half + 1) * TC],
                          start=(kd == 0), stop=stop)
            return te.matmul(out=ps_b[0:16, :],
                             lhsT=w_wqt[:, kd * 16:(kd + 1) * 16],
                             rhs=xhT[:, half * TC:(half + 1) * TC],
                             start=(kd == 0), stop=stop)
        smk = sweep_phase(src_buf, lnmk, zmm)

        def t_ev():
            zq_guard = trk["zq_gd"]

            def _(s):
                s.wait_ge(sems["st"], smk["m"][NDT - 1])
                s.wait_ge(sems["gd"], zq_guard)
                for mm in range(4):
                    s.activation(out=zq_sb[:, mm * TC:(mm + 1) * TC],
                                 in_=ps_z[:, mm * 512:(mm + 1) * 512],
                                 func=AF.Identity)
                s.activation(out=zq_sb[0:16, 4 * TC:5 * TC], in_=ps_b[0:16, :],
                             func=AF.Identity).then_inc(sems["ss"], 1)
                mk["ev"] = bump("ss")
                trk["qpsb"] = ("ss", mk["ev"])
            on("scalar", _)
        defer(t_ev, heavy=HEAVY_TAIL)

        def t_cc():
            ccm = {}

            def _(g):
                g.wait_ge(sems["ss"], mk["ev"])
                dma(g, bnc[(ch, tag, "i")].ap()[0:128, :], zq_sb[:, 0:4 * TC],
                    "gd")
                dma(g, bnc[(ch, tag, "i")].ap()[128:132, :].flatten()
                    .rearrange("(p t) -> p t", p=16),
                    zq_sb[0:16, 4 * TC:5 * TC], "gd")
                trk["zq_gd"] = cnt["gd"]
                g.wait_ge(sems["gd"], cnt["gd"])
                g.collective_compute(
                    "AllReduce", ALU.add, replica_groups=RG,
                    ins=[bnc[(ch, tag, "i")].ap()],
                    outs=[bnc[(ch, tag, "o")].ap()],
                ).then_inc(sems["cc"], 1)
                ccm["cc"] = bump("cc")
            on("gpsimd", _)

            def _(sy, zqr=trk["zqr_rd"], q16r=trk["q16_rd"],
                  zgr=trk["zgT_rd"]):
                sy.wait_ge(sems["cc"], ccm["cc"])
                sy.wait_ge(sems["sv"], zqr)
                sy.wait_ge(sems["st"], q16r)
                sy.wait_ge(sems["st"], zgr)
                dma(sy, zq_r[:, :], bnc[(ch, tag, "o")].ap()[0:128, :], "sd")
                dma(sy, q16[:, :], bnc[(ch, tag, "o")].ap()[128:132, :]
                    .flatten().rearrange("(p t) -> p t", p=16), "sd")
                mk["gd"] = cnt["sd"]
            on("sync", _)
        defer(t_cc, heavy=HEAVY_TAIL)

        def t_sc_mm():
            def _(te, qg=trk["qpsb"]):
                te.wait_ge(sems["sd"], mk["gd"])
                te.wait_ge(sems["ss"], mk["ev"])
                for tt in range(NTT):
                    ins = te.matmul(out=ps_b[:, tt * 64:(tt + 1) * 64],
                                    lhsT=q16[:, tt * 128:(tt + 1) * 128],
                                    rhs=w_keys[:, :], start=True, stop=True)
                ins.then_inc(sems["st"], 1)
                mk["sc"] = bump("st")
                trk["q16_rd"] = mk["sc"]
            on("tensor", _)
        defer(t_sc_mm, heavy=HEAVY_TAIL)

        def t_sc():
            def _(s, sgr=trk["scg"][mbuf]):
                s.wait_ge(sems["st"], mk["sc"])
                s.wait_ge(sems["sv"], sgr)
                s.activation(out=sc_g[:, sc0:sc0 + NTT * NEXP],
                             in_=ps_b[:, 0:NTT * NEXP],
                             func=AF.Identity).then_inc(sems["ss"], 1)
                mk["scev"] = bump("ss")
            on("scalar", _)

            def _(v):
                v.wait_ge(sems["ss"], mk["scev"])
                for tt in range(NTT):
                    sc_t = sc_g[:, sc0 + tt * NEXP:sc0 + (tt + 1) * NEXP]
                    for jc in range(4):
                        v.tensor_tensor(
                            out=cmp_g[:, :].rearrange("p (i k) -> p i k", k=16),
                            in0=sc_t[:, jc * 16:(jc + 1) * 16].unsqueeze(1)
                                .broadcast_to([128, NEXP, 16]),
                            in1=sc_t.unsqueeze(2).broadcast_to([128, NEXP, 16]),
                            op=ALU.is_ge)
                        ins = v.tensor_reduce(
                            out=(rk_g if jc == 0 else rk_t)[:, :],
                            in_=cmp_g[:, :].rearrange("p (i k) -> p i k", k=16),
                            axis=mybir.AxisListType.X, op=ALU.add)
                        if jc > 0:
                            ins.then_inc(sems["sv"], 1)
                            v.wait_ge(sems["sv"], bump("sv"))
                            ins = v.tensor_add(rk_g[:, :], rk_g[:, :],
                                               rk_t[:, :])
                    ins.then_inc(sems["sv"], 1)
                    v.wait_ge(sems["sv"], bump("sv"))
                    v.tensor_single_scalar(
                        msk16[:, sc0 + tt * NEXP:sc0 + (tt + 1) * NEXP],
                        rk_g[:, :], 16.5, ALU.is_le)
                    v.tensor_reduce(out=mt_g[:, 0:1], in_=sc_t,
                                    axis=mybir.AxisListType.X,
                                    op=ALU.max).then_inc(sems["sv"], 1)
                    v.wait_ge(sems["sv"], bump("sv"))
                    v.tensor_scalar_mul(gvec[:, gb + 8 * tt + 6:gb + 8 * tt + 7],
                                        mt_g[:, 0:1], -1.0).then_inc(
                                            sems["sv"], 1)
                    mk[f"g1_{tt}"] = bump("sv")
                trk["scg"][mbuf] = mk[f"g1_{NTT - 1}"]
            on("vector", _)
        defer(t_sc, heavy=HEAVY_TAIL)

        def t_gt():
            def _(s, gxr=trk["gex_rd"]):
                s.wait_ge(sems["sv"], gxr)
                for tt in range(NTT):
                    s.wait_ge(sems["sv"], mk[f"g1_{tt}"])
                    s.activation(out=gex[:, tt * NEXP:(tt + 1) * NEXP],
                                 in_=sc_g[:, sc0 + tt * NEXP:
                                          sc0 + (tt + 1) * NEXP],
                                 func=AF.Exp,
                                 bias=gvec[:, gb + 8 * tt + 6:
                                           gb + 8 * tt + 7]).then_inc(
                                     sems["ss"], 1)
                    mk[f"e{tt}"] = bump("ss")
            on("scalar", _)

            def _(v, gtr=trk["gate_rd"]):
                v.wait_ge(sems["st"], gtr)
                for tt in range(NTT):
                    gx = gex[:, tt * NEXP:(tt + 1) * NEXP]
                    v.wait_ge(sems["ss"], mk[f"e{tt}"])
                    v.tensor_mul(gx, gx,
                                 msk16[:, sc0 + tt * NEXP:
                                       sc0 + (tt + 1) * NEXP])
                    v.tensor_reduce(out=gvec[:, gb + 8 * tt + 7:
                                             gb + 8 * tt + 8], in_=gx,
                                    axis=mybir.AxisListType.X,
                                    op=ALU.add).then_inc(sems["sv"], 1)
                    v.wait_ge(sems["sv"], bump("sv"))
                    v.reciprocal(gvec[:, gb + 8 * tt + 6:gb + 8 * tt + 7],
                                 gvec[:, gb + 8 * tt + 7:
                                      gb + 8 * tt + 8]).then_inc(
                                     sems["sv"], 1)
                    v.wait_ge(sems["sv"], bump("sv"))
                    ins = v.tensor_scalar_mul(
                        gate[:, tt * NEXP:(tt + 1) * NEXP], gx,
                        gvec[:, gb + 8 * tt + 6:gb + 8 * tt + 7])
                ins.then_inc(sems["sv"], 1)
                mk["g2"] = bump("sv")
                trk["gex_rd"] = mk["g2"]
            on("vector", _)
        defer(t_gt, heavy=HEAVY_SWEEPS)

        def t_gr():
            def _(te, pg=trk["psytg"]):
                te.wait_ge(sems["sv"], mk["g2"])
                te.wait_ge(sems[pg[0]], pg[1])
                for tt in range(NTT):
                    ins = te.transpose(
                        out=ps_yt[0:64, TC + tt * 128:TC + (tt + 1) * 128],
                        in_=gate[:, tt * NEXP:(tt + 1) * NEXP],
                        identity=w_id16[:, :])
                ins.then_inc(sems["st"], 1)
                mk["gt"] = bump("st")
                trk["gate_rd"] = mk["gt"]
            on("tensor", _)

            def _(s, gTr=trk["gateT_rd"]):
                s.wait_ge(sems["st"], mk["gt"])
                s.wait_ge(sems["st"], gTr)
                s.activation(out=gateT[:, :], in_=ps_yt[0:64, TC:2 * TC],
                             func=AF.Identity).then_inc(sems["ss"], 1)
                mk["gte"] = bump("ss")
                trk["psytg"] = ("ss", mk["gte"])
            on("scalar", _)

            gr_marks = {}
            for mm in range(4):
                def _(te, mm=mm, gr=trk["grep_rd"]):
                    if mm == 0:
                        te.wait_ge(sems["ss"], mk["gte"])
                        te.wait_ge(sems["sv"], gr)
                    else:
                        te.wait_ge(sems["sv"], gr_marks[(mm - 1, "e")])
                    te.matmul(out=ps_b[:, :],
                              lhsT=w_sel[:, mm * 128:(mm + 1) * 128],
                              rhs=gateT[:, :], start=True,
                              stop=True).then_inc(sems["st"], 1)
                    gr_marks[(mm, "t")] = bump("st")
                on("tensor", _)

                def _(v, mm=mm):
                    v.wait_ge(sems["st"], gr_marks[(mm, "t")])
                    v.tensor_copy(grep[:, mm * TC:(mm + 1) * TC],
                                  ps_b[:, :]).then_inc(sems["sv"], 1)
                    gr_marks[(mm, "e")] = bump("sv")
                    trk["qpsb"] = ("sv", gr_marks[(mm, "e")])
                on("vector", _)
            mk["gr_marks"] = gr_marks
            trk["gateT_rd"] = gr_marks[(3, "t")]
        defer(t_gr, heavy=HEAVY_SWEEPS)

        def t_zg():
            def _(v, zr=trk["zgT_rd"]):
                v.wait_ge(sems["sv"], mk["gr_marks"][(3, "e")])
                v.wait_ge(sems["sd"], mk["gd"])
                v.wait_ge(sems["st"], zr)
                for mm in range(4):
                    ins = v.tensor_mul(zq_r[:, mm * TC:(mm + 1) * TC],
                                       zq_r[:, mm * TC:(mm + 1) * TC],
                                       grep[:, mm * TC:(mm + 1) * TC])
                ins.then_inc(sems["sv"], 1)
                mk["zg"] = bump("sv")
                trk["grep_rd"] = mk["zg"]
                trk["zqr_rd"] = mk["zg"]
            on("vector", _)
        defer(t_zg, heavy=HEAVY_SWEEPS)
        return mk

    def out2_phase(mk, store_cb):
        state = {"prev": None, "mm": None}
        for tt in range(NTT):
            def t_o(tt=tt):
                def _(te, tt=tt, prev=state["prev"]):
                    if tt == 0:
                        te.wait_ge(sems["sv"], mk["zg"])
                    else:
                        te.wait_ge(sems["sv"], prev)
                    for kb in range(4):
                        for q4 in range(4):
                            ins = te.matmul(
                                out=ps_z[:, q4 * 512:(q4 + 1) * 512],
                                lhsT=zq_r[:, kb * TC + tt * 128:
                                          kb * TC + (tt + 1) * 128],
                                rhs=w_bwt[:, kb * DH + q4 * 512:
                                          kb * DH + (q4 + 1) * 512],
                                start=(kb == 0), stop=(kb == 3))
                    ins.then_inc(sems["st"], 1)
                    state["mm"] = bump("st")
                    if tt == NTT - 1:
                        trk["zgT_rd"] = state["mm"]
                on("tensor", _)
                state["prev"] = store_cb(tt, state)
            defer(t_o, heavy=HEAVY_SWEEPS)
        return state

    def build_front(ch):
        fm = {}
        xbuf = ch % 2
        x1 = x1_sb[xbuf]
        t0 = ch * TC

        def t_load():
            free = trk["x1_free"][xbuf]

            def _(g, t0=t0):
                g.wait_ge(sems["sv"], prol_v)
                for tt in range(NTT):
                    if free is not None:
                        g.wait_ge(sems["gd"], free[tt])
                    dma(g, x1[:, tt * DH:(tt + 1) * DH],
                        x_in[t0 + tt * 128:t0 + (tt + 1) * 128, :], "gd")
                fm["xload"] = cnt["gd"]
            on("gpsimd", _)
        defer(t_load)

        lz = {"sv": prol_v}

        def t_depfix():
            lz["gd"] = fm["xload"]
        defer(t_depfix)
        ln1 = ln_phase(ch, "s1", xbuf, xbuf, lz, 0)
        ffn1 = ffn_phase(ch, xbuf, ln1, "zq1", 0)

        def store_x1(tt, state):
            def _(v, tt=tt):
                v.wait_ge(sems["st"], state["mm"])
                v.tensor_add(x1[:, tt * DH:(tt + 1) * DH], ps_z[:, :],
                             x1[:, tt * DH:(tt + 1) * DH]).then_inc(
                                 sems["sv"], 1)
            on("vector", _)
            m = bump("sv")
            if tt == NTT - 1:
                trk["psz_rd"] = m
                fm["x1"] = m
            return m
        out2_phase(ffn1, store_x1)

        if debug:
            def t_dbg():
                def _(g, t0=t0):
                    g.wait_ge(sems["sv"], fm["x1"])
                    for tt in range(NTT):
                        dma(g, dbg["dbg_x1"][t0 + tt * 128:
                                             t0 + (tt + 1) * 128, :],
                            x1[:, tt * DH:(tt + 1) * DH], "gd")
                on("gpsimd", _)
            defer(t_dbg)

        lz2 = {}

        def t_depfix2():
            lz2["sv"] = fm["x1"]
        defer(t_depfix2)
        ln2 = ln_phase(ch, "s2", xbuf, xbuf, lz2, 1)
        fm["ln2"] = ln2

        def xpmm(te, kd, half):
            if kd == 0:
                te.wait_ge(sems["ss"], trk["psa"])
                q = trk["qpsb"]
                te.wait_ge(sems[q[0]], q[1])
            te.matmul(out=ps_a[:, :],
                      lhsT=w_xpt[:, kd * 160:kd * 160 + 128],
                      rhs=xhT[:, half * TC:(half + 1) * TC],
                      start=(kd == 0), stop=(kd == NDT - 1))
            return te.matmul(out=ps_b[0:32, :],
                             lhsT=w_xpt[:, kd * 160 + 128:(kd + 1) * 160],
                             rhs=xhT[:, half * TC:(half + 1) * TC],
                             start=(kd == 0), stop=(kd == NDT - 1))
        def t_xev():
            def _(s):
                s.wait_ge(sems["st"], xsmk["m"][NDT - 1])
                s.activation(out=xp_d[:, :], in_=ps_a[:, :], func=AF.Identity)
                s.activation(out=xp_bc[:, :], in_=ps_b[0:32, :],
                             func=AF.Identity).then_inc(sems["ss"], 1)
                fm["xpev"] = bump("ss")
                trk["psa"] = fm["xpev"]
                trk["qpsb"] = ("ss", fm["xpev"])
            on("scalar", _)

            def _(g, ch=ch):
                g.wait_ge(sems["ss"], fm["xpev"])
                dma(g, bnc[(ch, "xp", "i")].ap()[0:128, :], xp_d[:, :], "gd")
                dma(g, bnc[(ch, "xp", "i")].ap()[128:160, :], xp_bc[:, :], "gd")
                g.wait_ge(sems["gd"], cnt["gd"])
                g.collective_compute(
                    "AllReduce", ALU.add, replica_groups=RG,
                    ins=[bnc[(ch, "xp", "i")].ap()],
                    outs=[bnc[(ch, "xp", "o")].ap()],
                ).then_inc(sems["cc"], 1)
                bump("cc")
                fm["xpcc"] = cnt["cc"]
            on("gpsimd", _)
        xsmk = sweep_phase(xbuf, ln2, xpmm, n_group=NDT, tail_cb=t_xev)

        rm = {}
        fm["rm"] = rm

        def t_rb():
            def _(g, ch=ch, dtg=fm.get("dt_guard", 0),
                  brg=fm.get("brep_guard", 0)):
                g.wait_ge(sems["cc"], fm["xpcc"])
                if dtg:
                    g.wait_ge(sems["st"], dtg)
                if brg:
                    g.wait_ge(sems["sv"], brg)
                xp_o = bnc[(ch, "xp", "o")].ap()
                dma(g, dtin[:, :], xp_o[0:128, :], "gd")
                dma(g, b_rep[0:1, :], xp_o[128:144, :].flatten().unsqueeze(0),
                    "gd")
                dma(g, c_rep[0:1, :], xp_o[144:160, :].flatten().unsqueeze(0),
                    "gd")
                k = 1
                while k < 128:
                    kk = min(k, 128 - k)
                    g.wait_ge(sems["gd"], cnt["gd"])
                    dma(g, b_rep[k:k + kk, :], b_rep[0:kk, :], "gd")
                    dma(g, c_rep[k:k + kk, :], c_rep[0:kk, :], "gd")
                    k += kk
                g.wait_ge(sems["gd"], cnt["gd"])
                g.sem_inc(sems["sg"], 1)
                rm["rep"] = bump("sg")
            on("gpsimd", _)
        defer(t_rb, heavy=True)
        return fm

    def build_back(ch, y_last_ref):
        xbuf = ch % 2
        x1 = x1_sb[xbuf]
        yT = xh_sb[xbuf]
        t0 = ch * TC
        bm = {}

        def t_x2():
            def _(v):
                v.wait_ge(sems["sv"], max(y_last_ref["y"]))
                for tt in range(NTT):
                    ins = v.tensor_add(x1[:, tt * DH:(tt + 1) * DH],
                                       x1[:, tt * DH:(tt + 1) * DH],
                                       yT[:, tt * DH:(tt + 1) * DH])
                ins.then_inc(sems["sv"], 1)
                bm["x2"] = bump("sv")
                trk["xh_rd_v"][xbuf] = bm["x2"]
            on("vector", _)
        defer(t_x2)

        if debug:
            def t_dbg2():
                def _(g, t0=t0):
                    g.wait_ge(sems["sv"], bm["x2"])
                    for tt in range(NTT):
                        dma(g, dbg["dbg_x2"][t0 + tt * 128:
                                             t0 + (tt + 1) * 128, :],
                            x1[:, tt * DH:(tt + 1) * DH], "gd")
                on("gpsimd", _)
            defer(t_dbg2)

        lz3 = {}

        def t_depfix3():
            lz3["sv"] = bm["x2"]
        defer(t_depfix3)
        ln3 = ln_phase(ch, "s3", xbuf, xbuf, lz3, 2)
        ffn2 = ffn_phase(ch, xbuf, ln3, "zq2", 1)

        def store_out(tt, state, t0=t0, xbuf=xbuf):
            def _(v, tt=tt):
                v.wait_ge(sems["st"], state["mm"])
                v.tensor_add(x1[:, tt * DH:(tt + 1) * DH], ps_z[:, :],
                             x1[:, tt * DH:(tt + 1) * DH]).then_inc(
                                 sems["sv"], 1)
            on("vector", _)
            m = bump("sv")
            if tt == NTT - 1:
                trk["psz_rd"] = m

            def _(g, tt=tt, m=m, t0=t0):
                g.wait_ge(sems["sv"], m)
                dma(g, out[t0 + tt * 128:t0 + (tt + 1) * 128, :],
                    x1[:, tt * DH:(tt + 1) * DH], "gd")
                if tt == 0:
                    trk["x1_free"][xbuf] = [None] * NTT
                trk["x1_free"][xbuf][tt] = cnt["gd"]
            on("gpsimd", _)
            return m
        out2_phase(ffn2, store_out)
        return bm

    gstate = {"gi": 0, "scanms": {}}

    def emit_scan_chunk(ch, fm, drain_per_dtile=DRAIN_PER_DTILE):
        xbuf = ch % 2
        xh = xh_sb[xbuf]
        yT = xh
        mkds = [dict() for _ in range(NDT)]
        rm = fm["rm"]
        dAv = dAb[:, :].rearrange("p (k t) -> p k t", t=SEGL)
        dBxv = dBxb[:, :].rearrange("p (k t) -> p k t", t=SEGL)
        hbv = hb[:, :].rearrange("p (k t) -> p k t", t=SEGL)
        brv = b_rep[:, :].rearrange("p (n t) -> p n t", t=TC)
        crv = c_rep[:, :].rearrange("p (n t) -> p n t", t=TC)

        def emit_prep(i):
            mkd = mkds[i]
            pL = i % 2

            def _(te, i=i, mkd=mkd, pL=pL,
                  pe=trk["pstr"][pL], pa=trk["psa"]):
                if i == 0:
                    te.wait_ge(sems["sg"], rm["rep"])
                    te.wait_ge(sems["ss"], fm["ln2"]["nm"])
                te.wait_ge(sems["ss"], pa)
                te.matmul(out=ps_a[:, :],
                          lhsT=w_dtwt[:, i * 128:(i + 1) * 128],
                          rhs=dtin[:, :], start=True,
                          stop=True).then_inc(sems["st"], 1)
                mkd["dt"] = bump("st")
                te.wait_ge(sems[pe[0]], pe[1])
                for tt in range(NTT):
                    ins = te.transpose(out=ps_tr[:, pL * 512 + tt * 128:
                                               pL * 512 + (tt + 1) * 128],
                                       in_=xh[:, tt * DH + i * 128:
                                              tt * DH + (i + 1) * 128],
                                       identity=w_id16[:, :])
                ins.then_inc(sems["st"], 1)
                mkd["x2tt"] = bump("st")
                trk["xh_rd"][xbuf] = mkd["x2tt"]
            on("tensor", _)

            def _(s, i=i, mkd=mkd, pL=pL, lxr=trk["Lx2_rd"][pL]):
                s.wait_ge(sems["st"], mkd["dt"])
                s.wait_ge(sems["sv"], lxr)
                s.activation(out=tsg[:, :], in_=ps_a[:, :], func=AF.Exp,
                             bias=w_dtbn[:, i:i + 1]).then_inc(sems["ss"], 1)
                s.wait_ge(sems["ss"], bump("ss"))
                s.activation(out=L_d[:, pL * TC:(pL + 1) * TC], in_=tsg[:, :],
                             func=AF.Ln, bias=1.0).then_inc(sems["ss"], 1)
                mkd["L"] = bump("ss")
                trk["psa"] = mkd["L"]
                s.wait_ge(sems["st"], mkd["x2tt"])
                s.activation(out=x2T_d[:, pL * TC:(pL + 1) * TC],
                             in_=ps_tr[:, pL * 512:(pL + 1) * 512],
                             func=AF.Identity).then_inc(sems["ss"], 1)
                mkd["x2te"] = bump("ss")
                trk["pstr"][pL] = ("ss", mkd["x2te"])
            on("scalar", _)

        def emit_da(i):
            mkd = mkds[i]
            pL = i % 2
            gi0 = gstate["gi"]
            guards = [gstate["scanms"].get(gi0 + g - 4) for g in range(NGPT)]

            def _(s, i=i, mkd=mkd, pL=pL, gi0=gi0, guards=guards):
                for g in range(NGPT):
                    if guards[g] is not None:
                        s.wait_ge(sems["sv"], guards[g])
                    base = ((gi0 + g) % 4) * NGRP * SEGL
                    for j in range(NGRP):
                        n = g * NGRP + j
                        ins = s.activation(
                            out=dAb[:, base + j * SEGL + 1:
                                    base + j * SEGL + 1 + TC],
                            in_=L_d[:, pL * TC:(pL + 1) * TC],
                            func=AF.Exp,
                            scale=w_absA[:, i * NST + n:i * NST + n + 1])
                    ins.then_inc(sems["ss"], 1)
                    mkd[f"dAg{g}"] = bump("ss")
            on("scalar", _)

        def emit_dve_tile(i):
            mkd = mkds[i]
            pL = i % 2
            gi0 = gstate["gi"]

            def _(v, i=i, mkd=mkd, pL=pL, gi0=gi0, rep=rm["rep"],
                  ytr=trk["ytot_rd"]):
                v.wait_ge(sems["ss"], mkd["x2te"])
                v.wait_ge(sems["sg"], rep)
                v.tensor_mul(u_d[:, :],
                             L_d[:, pL * TC:(pL + 1) * TC],
                             x2T_d[:, pL * TC:(pL + 1) * TC])
                for g in range(NGPT):
                    par = (gi0 + g) % 4
                    n0 = g * NGRP
                    c0 = i * NST + n0
                    v.tensor_copy(dBxv[:, :, 0:1],
                                  carry[:, c0:c0 + NGRP].unsqueeze(2))
                    v.tensor_tensor(
                        out=dBxv[:, :, 1:SEGL],
                        in0=u_d[:, :].unsqueeze(1).broadcast_to(
                            [128, NGRP, TC]),
                        in1=brv[:, n0:n0 + NGRP, :], op=ALU.mult)
                    v.wait_ge(sems["ss"], mkd[f"dAg{g}"])
                    v.tensor_tensor_scan(
                        out=hb[:, :],
                        data0=dAb[:, par * NGRP * SEGL:
                                  (par + 1) * NGRP * SEGL],
                        data1=dBxb[:, :], initial=zc[:, 0:1],
                        op0=ALU.mult, op1=ALU.add).then_inc(sems["sv"], 1)
                    gstate["scanms"][gi0 + g] = bump("sv")
                    v.wait_ge(sems["sv"], gstate["scanms"][gi0 + g])
                    v.tensor_copy(carry[:, c0:c0 + NGRP].unsqueeze(2),
                                  hbv[:, :, SEGL - 1:SEGL])
                    v.tensor_tensor(
                        out=dBxb[:, 0:NGRP * TC].rearrange(
                            "p (n t) -> p n t", t=TC),
                        in0=hbv[:, :, 1:SEGL],
                        in1=crv[:, n0:n0 + NGRP, :], op=ALU.mult)
                    v.tensor_add(dBxb[:, 0:2 * TC], dBxb[:, 0:2 * TC],
                                 dBxb[:, 2 * TC:4 * TC])
                    v.tensor_add(dBxb[:, 0:TC], dBxb[:, 0:TC],
                                 dBxb[:, TC:2 * TC])
                    if g == 0:
                        v.tensor_copy(yacc[:, :], dBxb[:, 0:TC])
                    else:
                        v.tensor_add(yacc[:, :], yacc[:, :], dBxb[:, 0:TC])
                v.wait_ge(sems["st"], ytr)
                v.scalar_tensor_tensor(out=ytot[:, :],
                                       in0=x2T_d[:, pL * TC:(pL + 1) * TC],
                                       scalar=w_D[:, i:i + 1],
                                       in1=yacc[:, :],
                                       op0=ALU.mult,
                                       op1=ALU.add).then_inc(sems["sv"], 1)
                mkd["y"] = bump("sv")
                trk["Lx2_rd"][pL] = mkd["y"]
            on("vector", _)
            gstate["gi"] = gi0 + NGPT

        def emit_yback(i):
            mkd = mkds[i]

            def _(te, mkd=mkd, pg=trk["psyt"]):
                te.wait_ge(sems["sv"], mkd["y"])
                te.wait_ge(sems[pg[0]], pg[1])
                for tt in range(NTT):
                    ins = te.transpose(out=ps_yt[:, tt * 128:(tt + 1) * 128],
                                       in_=ytot[:, tt * 128:(tt + 1) * 128],
                                       identity=w_id16[:, :])
                ins.then_inc(sems["st"], 1)
                mkd["yt"] = bump("st")
                trk["ytot_rd"] = mkd["yt"]
            on("tensor", _)

            def _(v, i=i, mkd=mkd):
                v.wait_ge(sems["st"], mkd["yt"])
                v.tensor_copy(yT[:, :].rearrange("p (tt c) -> p tt c", tt=NTT)
                              [:, :, i * 128:(i + 1) * 128],
                              ps_yt[:, 0:TC].rearrange(
                                  "p (tt c) -> p tt c", tt=NTT)).then_inc(
                                      sems["sv"], 1)
                mkd["yte"] = bump("sv")
                trk["psyt"] = ("sv", mkd["yte"])
            on("vector", _)

        if drain_per_dtile == 0:
            drain()
        else:
            drain(12, light_only=True)
        for i in range(NDT):
            emit_prep(i)
            if i >= 1:
                emit_yback(i - 1)
            emit_da(i)
            emit_dve_tile(i)
            drain(drain_per_dtile, light_only=True)
        emit_yback(NDT - 1)

        if debug and ch == 0:
            def _(g):
                g.wait_ge(sems["sv"], max(mkds[NDT - 2]["yte"],
                                          mkds[NDT - 1]["yte"]))
                dma(g, dbg["dbg_yt"], yT[:, :], "gd")
            on("gpsimd", _)

        y_last = [mkds[NDT - 2]["yte"], mkds[NDT - 1]["yte"]]
        return y_last, mkds[NDT - 1]["y"], mkds[NDT - 1]["dt"]

    fronts = [None] * (nch + 1)
    fronts[0] = build_front(0)
    drain()

    prev_y_last = None
    for ch in range(nch):
        fm = fronts[ch]
        if ch > 0:
            build_back(ch - 1, {"y": prev_y_last})
            if SERIAL_BACK:
                drain()
        if ch + 1 < nch:
            fronts[ch + 1] = build_front(ch + 1)
        y_last, last_v, last_dt = emit_scan_chunk(ch, fm)
        prev_y_last = y_last
        if ch + 1 < nch:
            fronts[ch + 1]["dt_guard"] = last_dt
            fronts[ch + 1]["brep_guard"] = last_v
        drain()

    build_back(nch - 1, {"y": prev_y_last})
    drain()

    def _(g):
        g.wait_ge(sems["gd"], cnt["gd"])
        g.wait_ge(sems["sd"], cnt["sd"])
    on("gpsimd", _)

    ctx.close()
    return nc


def _shard_inputs(inputs):
    xs = np.asarray(inputs["L0_attn_x"])
    A_log = np.asarray(inputs["L0_attn_A_log"])
    Dv = np.asarray(inputs["L0_attn_D"])
    xpw = np.asarray(inputs["L0_attn_x_proj_w"])
    dtw = np.asarray(inputs["L0_attn_dt_proj_w"])
    dtbv = np.asarray(inputs["L0_attn_dt_proj_b"])
    Aw = np.asarray(inputs["L0_ffn_A_w"])
    Bw = np.asarray(inputs["L0_ffn_B_w"])
    Wq = np.asarray(inputs["L0_ffn_Wq_w"])
    keys = np.asarray(inputs["L0_ffn_keys"])

    ident16 = np.eye(128, dtype=np.float16)
    ident32 = np.eye(128, dtype=np.float32)
    selm = np.zeros((64, 4 * 128), dtype=np.float16)
    for mm in range(4):
        for p in range(128):
            selm[(mm * 128 + p) // 8, mm * 128 + p] = 1.0

    maps = []
    for cid in range(8):
        k, j = cid // 2, cid % 2
        sl = slice(j * DH, (j + 1) * DH)
        maps.append(dict(
            x_in=np.ascontiguousarray(xs[k][:, sl]),
            awt=np.ascontiguousarray(Aw[:, sl].T),
            bwt=np.ascontiguousarray(Bw[sl, :].T),
            wqt=np.ascontiguousarray(Wq[:, sl].T),
            keyst=np.ascontiguousarray(keys.T),
            xpt=np.ascontiguousarray(xpw[:, sl].T),
            dtwt=np.ascontiguousarray(dtw[sl, :].T),
            dtb=np.ascontiguousarray(dtbv[sl].reshape(NDT, 128).T),
            alog=np.ascontiguousarray(
                A_log[sl].reshape(NDT, 128, NST).transpose(1, 0, 2)
                .reshape(128, NDT * NST)),
            dvec=np.ascontiguousarray(Dv[sl].reshape(NDT, 128).T),
            ident16=ident16, ident32=ident32, sel=selm,
        ))
    return maps


def kernel(**inputs):
    seq_len = 4096
    nc = build_program(seq_len)
    maps = _shard_inputs(inputs)
    res = run_bass_kernel_spmd(nc, maps, core_ids=list(range(8)))
    full = np.empty((4, seq_len, DMODEL), dtype=np.float32)
    for cid in range(8):
        k, j = cid // 2, cid % 2
        full[k][:, j * 2048:(j + 1) * 2048] = res.results[cid]["out"]
    return full


# version 17
# speedup vs baseline: 1.0140x; 1.0140x over previous
from contextlib import ExitStack

import numpy as np
import concourse.bass as bass
import concourse.mybir as mybir
from concourse.bass_utils import run_bass_kernel_spmd

F32 = mybir.dt.float32
FP16 = mybir.dt.float16
AF = mybir.ActivationFunctionType
ALU = mybir.AluOpType

DMODEL = 4096
DH = 2048
NDT = DH // 128
NST = 16
BNECK = 512
NEXP = 64
TC = 512
NTT = TC // 128
NGRP = 4
NGPT = NST // NGRP
SEGL = TC + 1
LN_EPS = 1e-5
HEAVY_SWEEPS = False
HEAVY_TAIL = False
SERIAL_BACK = False
PIN_TABLE = True
DRAIN_PER_DTILE = 3
RG = [[0, 1], [2, 3], [4, 5], [6, 7]]


def build_program(seq_len, debug=False):
    nch = seq_len // TC
    nc = bass.Bass("TRN2", target_bir_lowering=False, num_devices=8,
                   detect_race_conditions=False)

    def din(name, shape, dt=F32):
        return nc.dram_tensor(name, shape, dt, kind="ExternalInput").ap()

    x_in = din("x_in", [seq_len, DH])
    awt = din("awt", [DH, BNECK])
    bwt = din("bwt", [BNECK, DH])
    wqt = din("wqt", [DH, 16])
    keyst = din("keyst", [16, NEXP])
    xpt = din("xpt", [DH, 160])
    dtwt = din("dtwt", [128, DH])
    dtb = din("dtb", [128, NDT])
    alog = din("alog", [128, NDT * NST])
    dvec = din("dvec", [128, NDT])
    ident16 = din("ident16", [128, 128], FP16)
    ident32 = din("ident32", [128, 128], F32)
    sel = din("sel", [NEXP, 4 * 128], FP16)

    out = nc.dram_tensor("out", [seq_len, DH], F32, kind="ExternalOutput").ap()
    dbg = {}
    if debug:
        for nm, shape in [("dbg_x1", [seq_len, DH]), ("dbg_x2", [seq_len, DH]),
                          ("dbg_yt", [128, NTT * DH])]:
            dbg[nm] = nc.dram_tensor(nm, shape, F32, kind="ExternalOutput").ap()

    bnc = {}
    for ch in range(nch):
        for tag, shape, dt in [("s1", [128, 2 * NTT], F32),
                               ("s2", [128, 2 * NTT], F32),
                               ("s3", [128, 2 * NTT], F32),
                               ("zq1", [132, 4 * TC], FP16),
                               ("zq2", [132, 4 * TC], FP16),
                               ("xp", [160, TC], F32)]:
            bnc[(ch, tag, "i")] = nc.dram_tensor(f"b_{tag}_{ch}_i", shape, dt)
            bnc[(ch, tag, "o")] = nc.dram_tensor(f"b_{tag}_{ch}_o", shape, dt)

    ctx = ExitStack()
    sb = lambda n, s, d: ctx.enter_context(nc.sbuf_tensor(n, s, d))
    psm = lambda n, s, d: ctx.enter_context(nc.psum_tensor(n, s, d))

    w_awt = sb("w_awt", [128, NDT * BNECK], FP16)
    w_bwt = sb("w_bwt", [128, 4 * DH], FP16)
    w_wqt = sb("w_wqt", [128, NDT * 16], FP16)
    w_keys = sb("w_keys", [16, NEXP], FP16)
    w_xpt = sb("w_xpt", [128, NDT * 160], FP16)
    w_dtwt = sb("w_dtwt", [128, DH], FP16)
    w_dtbn = sb("w_dtbn", [128, NDT], F32)
    w_absA = sb("w_absA", [128, NDT * NST], F32)
    w_D = sb("w_D", [128, NDT], F32)
    w_id16 = sb("w_id16", [128, 128], FP16)
    w_sel = sb("w_sel", [64, 4 * 128], FP16)

    x1_sb = [sb("x1_a", [128, NTT * DH], FP16),
             sb("x1_b", [128, NTT * DH], FP16)]
    xh_sb = [sb("xh_a", [128, NTT * DH], FP16),
             sb("xh_b", [128, NTT * DH], FP16)]

    xhT = sb("xhT", [128, 2 * TC], FP16)
    stats = sb("stats", [128, 2 * NTT], F32)
    stats_r = sb("stats_r", [128, 2 * NTT], F32)
    mvec = sb("mvec", [128, 3 * 8 * NTT], F32)
    gvec = sb("gvec", [128, 2 * 8 * NTT], F32)
    zq_sb = sb("zq_sb", [128, 5 * TC], FP16)
    zq_r = sb("zq_r", [128, 4 * TC], FP16)
    q16 = sb("q16", [16, TC], FP16)
    sc_g = sb("sc_g", [128, 2 * NTT * NEXP], F32)
    cmp_g = sb("cmp_g", [128, NEXP * 32], FP16)
    rk_g = sb("rk_g", [128, NEXP], F32)
    rk_t = sb("rk_t", [128, NEXP], F32)
    mt_g = sb("mt_g", [128, 2], F32)
    msk16 = sb("msk16", [128, 2 * NTT * NEXP], F32)
    gex = sb("gex", [128, NTT * NEXP], F32)
    gate = sb("gate", [128, NTT * NEXP], FP16)
    gateT = sb("gateT", [64, TC], FP16)
    grep = sb("grep", [128, 4 * TC], FP16)
    xp_d = sb("xp_d", [128, TC], F32)
    xp_bc = sb("xp_bc", [32, TC], F32)
    dtin = sb("dtin", [128, TC], FP16)
    b_rep = sb("b_rep", [128, NST * TC], FP16)
    c_rep = sb("c_rep", [128, NST * TC], FP16)
    tsg = sb("tsg", [128, TC], F32)
    L_d = sb("L_d", [128, 2 * TC], FP16)
    u_d = sb("u_d", [128, TC], FP16)
    x2T_d = sb("x2T_d", [128, 2 * TC], FP16)
    dAb = sb("dAb", [128, 4 * NGRP * SEGL], FP16)
    dBxb = sb("dBxb", [128, NGRP * SEGL], FP16)
    hb = sb("hb", [128, NGRP * SEGL], FP16)
    yacc = sb("yacc", [128, TC], FP16)
    carry = sb("carry", [128, NDT * NST], FP16)
    zc = sb("zc", [128, 1], FP16)
    ytot = sb("ytot", [128, 2 * TC], FP16)

    ps_z = psm("ps_z", [128, 2048], F32)
    ps_a = psm("ps_a", [128, 512], F32)
    ps_b = psm("ps_b", [128, 512], F32)
    ps_tr = psm("ps_tr", [128, 1024], FP16)
    ps_yt = psm("ps_yt", [128, 1024], FP16)

    sems = {n: ctx.enter_context(nc.semaphore(n))
            for n in ["sd", "gd", "cc", "sv", "ss", "st", "sg"]}
    cnt = {n: 0 for n in sems}

    blk = ctx.enter_context(nc.Block())

    def on(eng, fn):
        getattr(blk, eng)(fn)

    def bump(name, k=1):
        cnt[name] += k
        return cnt[name]

    def dma(q, out_ap, in_ap, s="sd"):
        q.dma_start(out=out_ap, in_=in_ap).then_inc(sems[s], 16)
        return bump(s, 16)

    trk = {
        "pstr": [("ss", 0), ("ss", 0)],
        "xhT": [0, 0],
        "psyt": ("ss", 0),
        "psytg": ("ss", 0),
        "psz_rd": 0,
        "psa": 0,
        "zq_gd": 0,
        "stats_gd": 0,
        "statsr_rd": 0,
        "zgT_rd": 0,
        "grep_rd": 0,
        "qpsb": ("ss", 0),
        "xh_rd": [0, 0],
        "xh_rd_v": [0, 0],
        "x1_free": [None, None],
        "Lx2_rd": [0, 0],
        "ytot_rd": [0, 0],
        "dA_rd": None,
        "scg": [0, 0],
        "gex_rd": 0,
        "gate_rd": 0,
        "gateT_rd": 0,
        "zqr_rd": 0,
        "q16_rd": 0,
    }

    def _(g):
        for kd in range(NDT):
            dma(g, w_awt[:, kd * BNECK:(kd + 1) * BNECK],
                awt[kd * 128:(kd + 1) * 128, :], "gd")
        for kb in range(4):
            dma(g, w_bwt[:, kb * DH:(kb + 1) * DH],
                bwt[kb * 128:(kb + 1) * 128, :], "gd")
        for kd in range(NDT):
            dma(g, w_wqt[:, kd * 16:(kd + 1) * 16],
                wqt[kd * 128:(kd + 1) * 128, :], "gd")
        dma(g, w_keys[:, :], keyst, "gd")
        for kd in range(NDT):
            dma(g, w_xpt[:, kd * 160:(kd + 1) * 160],
                xpt[kd * 128:(kd + 1) * 128, :], "gd")
        dma(g, w_dtwt[:, :], dtwt, "gd")
        dma(g, w_id16[:, :], ident16, "gd")
        dma(g, w_sel[:, :], sel, "gd")
        dma(g, w_dtbn[:, :], dtb, "gd")
        dma(g, w_absA[:, :], alog, "gd")
        dma(g, w_D[:, :], dvec, "gd")
    on("gpsimd", _)
    wmark = cnt["gd"]

    def _(v):
        v.wait_ge(sems["gd"], wmark)
        v.memset(carry[:, :], 0.0)
        v.memset(zc[:, :], 0.0)
        v.memset(dAb[:, :].rearrange("p (k t) -> p k t", t=SEGL)[:, :, 0:1],
                 0.0).then_inc(sems["sv"], 1)
    on("vector", _)
    prol_v = bump("sv")

    def _(s):
        if PIN_TABLE:
            s.add_instruction(mybir.InstLoadActFuncSet(
                name=nc.get_next_instruction_name(), act_func_set_id=6,
                ins=[], outs=[]))
        s.wait_ge(sems["gd"], wmark)
        s.activation(out=w_absA[:, :], in_=w_absA[:, :],
                     func=AF.Exp).then_inc(sems["ss"], 1)
        s.wait_ge(sems["ss"], bump("ss"))
        s.activation(out=w_absA[:, :], in_=w_absA[:, :], func=AF.Identity,
                     scale=-1.0).then_inc(sems["ss"], 1)
    on("scalar", _)
    prol_s = bump("ss")

    def _(te):
        te.wait_ge(sems["gd"], wmark)
    on("tensor", _)

    taskq = []

    def defer(fn, heavy=False):
        fn._heavy = heavy
        taskq.append(fn)

    def drain(k=None, light_only=False):
        n = len(taskq) if k is None else min(k, len(taskq))
        for _ in range(n):
            if light_only and getattr(taskq[0], "_heavy", False):
                return
            taskq.pop(0)()

    def ln_phase(ch, tag, xbuf, dst, dep, plane):
        mk = {}
        x1 = x1_sb[xbuf]
        xh = xh_sb[dst]
        lnv = mvec[:, plane * 8 * NTT:(plane + 1) * 8 * NTT]

        def t1():
            gd_guard = trk["stats_gd"]
            xh_guard = trk["xh_rd"][dst]
            xh_guard_v = trk["xh_rd_v"][dst]

            def _(v):
                v.wait_ge(sems["sv"], dep["sv"])
                if dep.get("gd"):
                    v.wait_ge(sems["gd"], dep["gd"])
                v.wait_ge(sems["gd"], gd_guard)
                for tt in range(NTT):
                    ins = v.tensor_reduce(out=stats[:, 2 * tt:2 * tt + 1],
                                          in_=x1[:, tt * DH:(tt + 1) * DH],
                                          axis=mybir.AxisListType.X, op=ALU.add)
                ins.then_inc(sems["sv"], 1)
                mk["v"] = bump("sv")
            on("vector", _)

            def _(s):
                s.wait_ge(sems["sv"], dep["sv"])
                if dep.get("gd"):
                    s.wait_ge(sems["gd"], dep["gd"])
                s.wait_ge(sems["st"], xh_guard)
                s.wait_ge(sems["sv"], xh_guard_v)
                s.wait_ge(sems["ss"], prol_s)
                for tt in range(NTT):
                    ins = s.activation(out=xh[:, tt * DH:(tt + 1) * DH],
                                       in_=x1[:, tt * DH:(tt + 1) * DH],
                                       func=AF.Square)
                ins.then_inc(sems["ss"], 1)
                mk["s"] = bump("ss")
            on("scalar", _)

            def _(v):
                v.wait_ge(sems["ss"], mk["s"])
                for tt in range(NTT):
                    ins = v.tensor_reduce(out=stats[:, 2 * tt + 1:2 * tt + 2],
                                          in_=xh[:, tt * DH:(tt + 1) * DH],
                                          axis=mybir.AxisListType.X, op=ALU.add)
                ins.then_inc(sems["sv"], 1)
                mk["s2v"] = bump("sv")
            on("vector", _)
        defer(t1)

        def t2():
            sr_guard = trk["statsr_rd"]
            ccm = {}

            def _(g):
                g.wait_ge(sems["sv"], mk["v"])
                g.wait_ge(sems["sv"], mk["s2v"])
                dma(g, bnc[(ch, tag, "i")].ap(), stats[:, :], "gd")
                trk["stats_gd"] = cnt["gd"]
                g.wait_ge(sems["gd"], cnt["gd"])
                g.collective_compute(
                    "AllReduce", ALU.add, replica_groups=RG,
                    ins=[bnc[(ch, tag, "i")].ap()],
                    outs=[bnc[(ch, tag, "o")].ap()],
                ).then_inc(sems["cc"], 1)
                ccm["cc"] = bump("cc")
            on("gpsimd", _)

            def _(sy):
                sy.wait_ge(sems["cc"], ccm["cc"])
                sy.wait_ge(sems["sv"], sr_guard)
                dma(sy, stats_r[:, :], bnc[(ch, tag, "o")].ap(), "sd")
                mk["gd"] = cnt["sd"]
            on("sync", _)
        defer(t2)

        def t3():
            def _(v):
                v.wait_ge(sems["sd"], mk["gd"])
                mm = lnv.rearrange("p (t k) -> p t k", k=8)
                m_, v_ = mm[:, :, 0:1], mm[:, :, 1:2]
                sq_ = mm[:, :, 4:5]
                sr = stats_r[:, :].rearrange("p (t k) -> p t k", k=2)
                v.tensor_scalar_mul(m_, sr[:, :, 0:1], 1.0 / DMODEL)
                v.tensor_scalar_mul(v_, sr[:, :, 1:2], 1.0 / DMODEL).then_inc(
                    sems["sv"], 1)
                v.wait_ge(sems["sv"], bump("sv"))
                trk["statsr_rd"] = cnt["sv"]
                v.tensor_mul(sq_, m_, m_).then_inc(sems["sv"], 1)
                v.wait_ge(sems["sv"], bump("sv"))
                v.tensor_sub(v_, v_, sq_).then_inc(sems["sv"], 1)
                v.wait_ge(sems["sv"], bump("sv"))
                v.tensor_scalar_add(v_, v_, LN_EPS).then_inc(sems["sv"], 1)
                mk["v2"] = bump("sv")
            on("vector", _)

            def _(s):
                s.wait_ge(sems["sv"], mk["v2"])
                mm = lnv.rearrange("p (t k) -> p t k", k=8)
                s.activation(out=mm[:, :, 6:7], in_=mm[:, :, 1:2],
                             func=AF.Ln).then_inc(sems["ss"], 1)
                s.wait_ge(sems["ss"], bump("ss"))
                s.activation(out=mm[:, :, 2:3], in_=mm[:, :, 6:7],
                             func=AF.Exp, scale=-0.5).then_inc(sems["ss"], 1)
                mk["s2"] = bump("ss")
            on("scalar", _)

            def _(v):
                v.wait_ge(sems["ss"], mk["s2"])
                mm = lnv.rearrange("p (t k) -> p t k", k=8)
                m_, i_, nmi = mm[:, :, 0:1], mm[:, :, 2:3], mm[:, :, 3:4]
                v.scalar_tensor_tensor(out=nmi, in0=m_, scalar=-1.0, in1=i_,
                                       op0=ALU.mult,
                                       op1=ALU.mult).then_inc(sems["sv"], 1)
                mk["v3"] = bump("sv")
            on("vector", _)

            def _(v):
                v.wait_ge(sems["sv"], mk["v3"])
                for tt in range(NTT):
                    ins = v.tensor_scalar(out=xh[:, tt * DH:(tt + 1) * DH],
                                          in0=x1[:, tt * DH:(tt + 1) * DH],
                                          scalar1=lnv[:, 8 * tt + 2:8 * tt + 3],
                                          scalar2=lnv[:, 8 * tt + 3:8 * tt + 4],
                                          op0=ALU.mult, op1=ALU.add)
                ins.then_inc(sems["sv"], 1)
                mk["nmv"] = bump("sv")
            on("vector", _)

            def _(s):
                s.wait_ge(sems["sv"], mk["nmv"])
                s.sem_inc(sems["ss"], 1)
                mk["nm"] = bump("ss")
            on("scalar", _)
        defer(t3)
        return mk

    def sweep_phase(src_buf, lnmk, mm_emit, n_group=4, tail_cb=None):
        smk = {"t": {}, "e": {}, "m": {}}
        src = xh_sb[src_buf]
        for g0 in range(0, NDT, n_group):
            def tk(g0=g0):
                for kd in range(g0, g0 + n_group):
                    half = kd % 2

                    def _(te, kd=kd, half=half, pe=trk["pstr"][half]):
                        if kd == 0:
                            te.wait_ge(sems["ss"], lnmk["nm"])
                        te.wait_ge(sems[pe[0]], pe[1])
                        for tt in range(NTT):
                            ins = te.transpose(
                                out=ps_tr[:, half * 512 + tt * 128:
                                          half * 512 + (tt + 1) * 128],
                                in_=src[:, tt * DH + kd * 128:
                                        tt * DH + (kd + 1) * 128],
                                identity=w_id16[:, :])
                        ins.then_inc(sems["st"], 1)
                        smk["t"][kd] = bump("st")
                        trk["xh_rd"][src_buf] = smk["t"][kd]
                    on("tensor", _)

                    def _(v, kd=kd, half=half, xr=trk["xhT"][half]):
                        v.wait_ge(sems["st"], smk["t"][kd])
                        v.wait_ge(sems["st"], xr)
                        v.tensor_copy(xhT[:, half * TC:(half + 1) * TC],
                                      ps_tr[:, half * 512:
                                            (half + 1) * 512]).then_inc(
                                          sems["sv"], 1)
                        smk["e"][kd] = bump("sv")
                        trk["pstr"][half] = ("sv", smk["e"][kd])
                    on("vector", _)

                    def _(te, kd=kd, half=half):
                        te.wait_ge(sems["sv"], smk["e"][kd])
                        ins = mm_emit(te, kd, half)
                        ins.then_inc(sems["st"], 1)
                        smk["m"][kd] = bump("st")
                        trk["xhT"][half] = smk["m"][kd]
                    on("tensor", _)
                if tail_cb is not None and g0 + n_group >= NDT:
                    tail_cb()
            defer(tk, heavy=HEAVY_TAIL)
        return smk

    def ffn_phase(ch, src_buf, lnmk, tag, mbuf):
        mk = {}
        sc0 = mbuf * NTT * NEXP
        gb = mbuf * 8 * NTT

        def zmm(te, kd, half):
            if kd == 0:
                te.wait_ge(sems["sv"], trk["psz_rd"])
                q = trk["qpsb"]
                te.wait_ge(sems[q[0]], q[1])
            stop = (kd % 4 == 3)
            for mm in range(4):
                te.matmul(out=ps_z[:, mm * 512:(mm + 1) * 512],
                          lhsT=w_awt[:, kd * BNECK + mm * 128:
                                     kd * BNECK + (mm + 1) * 128],
                          rhs=xhT[:, half * TC:(half + 1) * TC],
                          start=(kd == 0), stop=stop)
            return te.matmul(out=ps_b[0:16, :],
                             lhsT=w_wqt[:, kd * 16:(kd + 1) * 16],
                             rhs=xhT[:, half * TC:(half + 1) * TC],
                             start=(kd == 0), stop=stop)
        smk = sweep_phase(src_buf, lnmk, zmm)

        def t_ev():
            zq_guard = trk["zq_gd"]

            def _(s):
                s.wait_ge(sems["st"], smk["m"][NDT - 1])
                s.wait_ge(sems["gd"], zq_guard)
                for mm in range(4):
                    s.activation(out=zq_sb[:, mm * TC:(mm + 1) * TC],
                                 in_=ps_z[:, mm * 512:(mm + 1) * 512],
                                 func=AF.Identity)
                s.activation(out=zq_sb[0:16, 4 * TC:5 * TC], in_=ps_b[0:16, :],
                             func=AF.Identity).then_inc(sems["ss"], 1)
                mk["ev"] = bump("ss")
                trk["qpsb"] = ("ss", mk["ev"])
            on("scalar", _)
        defer(t_ev, heavy=HEAVY_TAIL)

        def t_cc():
            ccm = {}

            def _(g):
                g.wait_ge(sems["ss"], mk["ev"])
                dma(g, bnc[(ch, tag, "i")].ap()[0:128, :], zq_sb[:, 0:4 * TC],
                    "gd")
                dma(g, bnc[(ch, tag, "i")].ap()[128:132, :].flatten()
                    .rearrange("(p t) -> p t", p=16),
                    zq_sb[0:16, 4 * TC:5 * TC], "gd")
                trk["zq_gd"] = cnt["gd"]
                g.wait_ge(sems["gd"], cnt["gd"])
                g.collective_compute(
                    "AllReduce", ALU.add, replica_groups=RG,
                    ins=[bnc[(ch, tag, "i")].ap()],
                    outs=[bnc[(ch, tag, "o")].ap()],
                ).then_inc(sems["cc"], 1)
                ccm["cc"] = bump("cc")
            on("gpsimd", _)

            def _(sy, zqr=trk["zqr_rd"], q16r=trk["q16_rd"],
                  zgr=trk["zgT_rd"]):
                sy.wait_ge(sems["cc"], ccm["cc"])
                sy.wait_ge(sems["sv"], zqr)
                sy.wait_ge(sems["st"], q16r)
                sy.wait_ge(sems["st"], zgr)
                dma(sy, zq_r[:, :], bnc[(ch, tag, "o")].ap()[0:128, :], "sd")
                dma(sy, q16[:, :], bnc[(ch, tag, "o")].ap()[128:132, :]
                    .flatten().rearrange("(p t) -> p t", p=16), "sd")
                mk["gd"] = cnt["sd"]
            on("sync", _)
        defer(t_cc, heavy=HEAVY_TAIL)

        def t_sc_mm():
            def _(te, qg=trk["qpsb"]):
                te.wait_ge(sems["sd"], mk["gd"])
                te.wait_ge(sems["ss"], mk["ev"])
                for tt in range(NTT):
                    ins = te.matmul(out=ps_b[:, tt * 64:(tt + 1) * 64],
                                    lhsT=q16[:, tt * 128:(tt + 1) * 128],
                                    rhs=w_keys[:, :], start=True, stop=True)
                ins.then_inc(sems["st"], 1)
                mk["sc"] = bump("st")
                trk["q16_rd"] = mk["sc"]
            on("tensor", _)
        defer(t_sc_mm, heavy=HEAVY_TAIL)

        def t_sc():
            def _(s, sgr=trk["scg"][mbuf]):
                s.wait_ge(sems["st"], mk["sc"])
                s.wait_ge(sems["sv"], sgr)
                s.activation(out=sc_g[:, sc0:sc0 + NTT * NEXP],
                             in_=ps_b[:, 0:NTT * NEXP],
                             func=AF.Identity).then_inc(sems["ss"], 1)
                mk["scev"] = bump("ss")
            on("scalar", _)

            def _(v):
                v.wait_ge(sems["ss"], mk["scev"])
                for tt in range(NTT):
                    sc_t = sc_g[:, sc0 + tt * NEXP:sc0 + (tt + 1) * NEXP]
                    for jc in range(2):
                        v.tensor_tensor(
                            out=cmp_g[:, :].rearrange("p (i k) -> p i k", k=32),
                            in0=sc_t[:, jc * 32:(jc + 1) * 32].unsqueeze(1)
                                .broadcast_to([128, NEXP, 32]),
                            in1=sc_t.unsqueeze(2).broadcast_to([128, NEXP, 32]),
                            op=ALU.is_ge)
                        ins = v.tensor_reduce(
                            out=(rk_g if jc == 0 else rk_t)[:, :],
                            in_=cmp_g[:, :].rearrange("p (i k) -> p i k", k=32),
                            axis=mybir.AxisListType.X, op=ALU.add)
                    ins.then_inc(sems["sv"], 1)
                    v.wait_ge(sems["sv"], bump("sv"))
                    v.tensor_add(rk_g[:, :], rk_g[:, :],
                                 rk_t[:, :]).then_inc(sems["sv"], 1)
                    v.wait_ge(sems["sv"], bump("sv"))
                    v.tensor_single_scalar(
                        msk16[:, sc0 + tt * NEXP:sc0 + (tt + 1) * NEXP],
                        rk_g[:, :], 16.5, ALU.is_le)
                    v.tensor_reduce(out=mt_g[:, 0:1], in_=sc_t,
                                    axis=mybir.AxisListType.X,
                                    op=ALU.max).then_inc(sems["sv"], 1)
                    v.wait_ge(sems["sv"], bump("sv"))
                    v.tensor_scalar_mul(gvec[:, gb + 8 * tt + 6:gb + 8 * tt + 7],
                                        mt_g[:, 0:1], -1.0).then_inc(
                                            sems["sv"], 1)
                    mk[f"g1_{tt}"] = bump("sv")
                trk["scg"][mbuf] = mk[f"g1_{NTT - 1}"]
            on("vector", _)
        defer(t_sc, heavy=HEAVY_TAIL)

        def t_gt():
            def _(s, gxr=trk["gex_rd"]):
                s.wait_ge(sems["sv"], gxr)
                for tt in range(NTT):
                    s.wait_ge(sems["sv"], mk[f"g1_{tt}"])
                    s.activation(out=gex[:, tt * NEXP:(tt + 1) * NEXP],
                                 in_=sc_g[:, sc0 + tt * NEXP:
                                          sc0 + (tt + 1) * NEXP],
                                 func=AF.Exp,
                                 bias=gvec[:, gb + 8 * tt + 6:
                                           gb + 8 * tt + 7]).then_inc(
                                     sems["ss"], 1)
                    mk[f"e{tt}"] = bump("ss")
            on("scalar", _)

            def _(v, gtr=trk["gate_rd"]):
                v.wait_ge(sems["st"], gtr)
                for tt in range(NTT):
                    gx = gex[:, tt * NEXP:(tt + 1) * NEXP]
                    v.wait_ge(sems["ss"], mk[f"e{tt}"])
                    v.tensor_mul(gx, gx,
                                 msk16[:, sc0 + tt * NEXP:
                                       sc0 + (tt + 1) * NEXP])
                    v.tensor_reduce(out=gvec[:, gb + 8 * tt + 7:
                                             gb + 8 * tt + 8], in_=gx,
                                    axis=mybir.AxisListType.X,
                                    op=ALU.add).then_inc(sems["sv"], 1)
                    v.wait_ge(sems["sv"], bump("sv"))
                    v.reciprocal(gvec[:, gb + 8 * tt + 6:gb + 8 * tt + 7],
                                 gvec[:, gb + 8 * tt + 7:
                                      gb + 8 * tt + 8]).then_inc(
                                     sems["sv"], 1)
                    v.wait_ge(sems["sv"], bump("sv"))
                    ins = v.tensor_scalar_mul(
                        gate[:, tt * NEXP:(tt + 1) * NEXP], gx,
                        gvec[:, gb + 8 * tt + 6:gb + 8 * tt + 7])
                ins.then_inc(sems["sv"], 1)
                mk["g2"] = bump("sv")
                trk["gex_rd"] = mk["g2"]
            on("vector", _)
        defer(t_gt, heavy=HEAVY_SWEEPS)

        def t_gr():
            def _(te, pg=trk["psytg"]):
                te.wait_ge(sems["sv"], mk["g2"])
                te.wait_ge(sems[pg[0]], pg[1])
                for tt in range(NTT):
                    ins = te.transpose(
                        out=ps_yt[0:64, TC + tt * 128:TC + (tt + 1) * 128],
                        in_=gate[:, tt * NEXP:(tt + 1) * NEXP],
                        identity=w_id16[:, :])
                ins.then_inc(sems["st"], 1)
                mk["gt"] = bump("st")
                trk["gate_rd"] = mk["gt"]
            on("tensor", _)

            def _(s, gTr=trk["gateT_rd"]):
                s.wait_ge(sems["st"], mk["gt"])
                s.wait_ge(sems["st"], gTr)
                s.activation(out=gateT[:, :], in_=ps_yt[0:64, TC:2 * TC],
                             func=AF.Identity).then_inc(sems["ss"], 1)
                mk["gte"] = bump("ss")
                trk["psytg"] = ("ss", mk["gte"])
            on("scalar", _)

            gr_marks = {}
            for mm in range(4):
                def _(te, mm=mm, gr=trk["grep_rd"]):
                    if mm == 0:
                        te.wait_ge(sems["ss"], mk["gte"])
                        te.wait_ge(sems["sv"], gr)
                    else:
                        te.wait_ge(sems["sv"], gr_marks[(mm - 1, "e")])
                    te.matmul(out=ps_b[:, :],
                              lhsT=w_sel[:, mm * 128:(mm + 1) * 128],
                              rhs=gateT[:, :], start=True,
                              stop=True).then_inc(sems["st"], 1)
                    gr_marks[(mm, "t")] = bump("st")
                on("tensor", _)

                def _(v, mm=mm):
                    v.wait_ge(sems["st"], gr_marks[(mm, "t")])
                    v.tensor_copy(grep[:, mm * TC:(mm + 1) * TC],
                                  ps_b[:, :]).then_inc(sems["sv"], 1)
                    gr_marks[(mm, "e")] = bump("sv")
                    trk["qpsb"] = ("sv", gr_marks[(mm, "e")])
                on("vector", _)
            mk["gr_marks"] = gr_marks
            trk["gateT_rd"] = gr_marks[(3, "t")]
        defer(t_gr, heavy=HEAVY_SWEEPS)

        def t_zg():
            def _(v, zr=trk["zgT_rd"]):
                v.wait_ge(sems["sv"], mk["gr_marks"][(3, "e")])
                v.wait_ge(sems["sd"], mk["gd"])
                v.wait_ge(sems["st"], zr)
                for mm in range(4):
                    ins = v.tensor_mul(zq_r[:, mm * TC:(mm + 1) * TC],
                                       zq_r[:, mm * TC:(mm + 1) * TC],
                                       grep[:, mm * TC:(mm + 1) * TC])
                ins.then_inc(sems["sv"], 1)
                mk["zg"] = bump("sv")
                trk["grep_rd"] = mk["zg"]
                trk["zqr_rd"] = mk["zg"]
            on("vector", _)
        defer(t_zg, heavy=HEAVY_SWEEPS)
        return mk

    def out2_phase(mk, store_cb):
        state = {"prev": None, "mm": None}
        for tt in range(NTT):
            def t_o(tt=tt):
                def _(te, tt=tt, prev=state["prev"]):
                    if tt == 0:
                        te.wait_ge(sems["sv"], mk["zg"])
                    else:
                        te.wait_ge(sems["sv"], prev)
                    for kb in range(4):
                        for q4 in range(4):
                            ins = te.matmul(
                                out=ps_z[:, q4 * 512:(q4 + 1) * 512],
                                lhsT=zq_r[:, kb * TC + tt * 128:
                                          kb * TC + (tt + 1) * 128],
                                rhs=w_bwt[:, kb * DH + q4 * 512:
                                          kb * DH + (q4 + 1) * 512],
                                start=(kb == 0), stop=(kb == 3))
                    ins.then_inc(sems["st"], 1)
                    state["mm"] = bump("st")
                    if tt == NTT - 1:
                        trk["zgT_rd"] = state["mm"]
                on("tensor", _)
                state["prev"] = store_cb(tt, state)
            defer(t_o, heavy=HEAVY_SWEEPS)
        return state

    def build_front(ch):
        fm = {}
        xbuf = ch % 2
        x1 = x1_sb[xbuf]
        t0 = ch * TC

        def t_load():
            free = trk["x1_free"][xbuf]

            def _(g, t0=t0):
                g.wait_ge(sems["sv"], prol_v)
                for tt in range(NTT):
                    if free is not None:
                        g.wait_ge(sems["gd"], free[tt])
                    dma(g, x1[:, tt * DH:(tt + 1) * DH],
                        x_in[t0 + tt * 128:t0 + (tt + 1) * 128, :], "gd")
                fm["xload"] = cnt["gd"]
            on("gpsimd", _)
        defer(t_load)

        lz = {"sv": prol_v}

        def t_depfix():
            lz["gd"] = fm["xload"]
        defer(t_depfix)
        ln1 = ln_phase(ch, "s1", xbuf, xbuf, lz, 0)
        ffn1 = ffn_phase(ch, xbuf, ln1, "zq1", 0)

        def store_x1(tt, state):
            def _(v, tt=tt):
                v.wait_ge(sems["st"], state["mm"])
                v.tensor_add(x1[:, tt * DH:(tt + 1) * DH], ps_z[:, :],
                             x1[:, tt * DH:(tt + 1) * DH]).then_inc(
                                 sems["sv"], 1)
            on("vector", _)
            m = bump("sv")
            if tt == NTT - 1:
                trk["psz_rd"] = m
                fm["x1"] = m
            return m
        out2_phase(ffn1, store_x1)

        if debug:
            def t_dbg():
                def _(g, t0=t0):
                    g.wait_ge(sems["sv"], fm["x1"])
                    for tt in range(NTT):
                        dma(g, dbg["dbg_x1"][t0 + tt * 128:
                                             t0 + (tt + 1) * 128, :],
                            x1[:, tt * DH:(tt + 1) * DH], "gd")
                on("gpsimd", _)
            defer(t_dbg)

        lz2 = {}

        def t_depfix2():
            lz2["sv"] = fm["x1"]
        defer(t_depfix2)
        ln2 = ln_phase(ch, "s2", xbuf, xbuf, lz2, 1)
        fm["ln2"] = ln2

        def xpmm(te, kd, half):
            if kd == 0:
                te.wait_ge(sems["ss"], trk["psa"])
                q = trk["qpsb"]
                te.wait_ge(sems[q[0]], q[1])
            te.matmul(out=ps_a[:, :],
                      lhsT=w_xpt[:, kd * 160:kd * 160 + 128],
                      rhs=xhT[:, half * TC:(half + 1) * TC],
                      start=(kd == 0), stop=(kd == NDT - 1))
            return te.matmul(out=ps_b[0:32, :],
                             lhsT=w_xpt[:, kd * 160 + 128:(kd + 1) * 160],
                             rhs=xhT[:, half * TC:(half + 1) * TC],
                             start=(kd == 0), stop=(kd == NDT - 1))
        def t_xev():
            def _(s):
                s.wait_ge(sems["st"], xsmk["m"][NDT - 1])
                s.activation(out=xp_d[:, :], in_=ps_a[:, :], func=AF.Identity)
                s.activation(out=xp_bc[:, :], in_=ps_b[0:32, :],
                             func=AF.Identity).then_inc(sems["ss"], 1)
                fm["xpev"] = bump("ss")
                trk["psa"] = fm["xpev"]
                trk["qpsb"] = ("ss", fm["xpev"])
            on("scalar", _)

            def _(g, ch=ch):
                g.wait_ge(sems["ss"], fm["xpev"])
                dma(g, bnc[(ch, "xp", "i")].ap()[0:128, :], xp_d[:, :], "gd")
                dma(g, bnc[(ch, "xp", "i")].ap()[128:160, :], xp_bc[:, :], "gd")
                g.wait_ge(sems["gd"], cnt["gd"])
                g.collective_compute(
                    "AllReduce", ALU.add, replica_groups=RG,
                    ins=[bnc[(ch, "xp", "i")].ap()],
                    outs=[bnc[(ch, "xp", "o")].ap()],
                ).then_inc(sems["cc"], 1)
                bump("cc")
                fm["xpcc"] = cnt["cc"]
            on("gpsimd", _)
        xsmk = sweep_phase(xbuf, ln2, xpmm, n_group=NDT, tail_cb=t_xev)

        rm = {}
        fm["rm"] = rm

        def t_rb():
            def _(g, ch=ch, dtg=fm.get("dt_guard", 0),
                  brg=fm.get("brep_guard", 0)):
                g.wait_ge(sems["cc"], fm["xpcc"])
                if dtg:
                    g.wait_ge(sems["st"], dtg)
                if brg:
                    g.wait_ge(sems["sv"], brg)
                xp_o = bnc[(ch, "xp", "o")].ap()
                dma(g, dtin[:, :], xp_o[0:128, :], "gd")
                dma(g, b_rep[0:1, :], xp_o[128:144, :].flatten().unsqueeze(0),
                    "gd")
                dma(g, c_rep[0:1, :], xp_o[144:160, :].flatten().unsqueeze(0),
                    "gd")
                k = 1
                while k < 128:
                    kk = min(k, 128 - k)
                    g.wait_ge(sems["gd"], cnt["gd"])
                    dma(g, b_rep[k:k + kk, :], b_rep[0:kk, :], "gd")
                    dma(g, c_rep[k:k + kk, :], c_rep[0:kk, :], "gd")
                    k += kk
                g.wait_ge(sems["gd"], cnt["gd"])
                g.sem_inc(sems["sg"], 1)
                rm["rep"] = bump("sg")
            on("gpsimd", _)
        defer(t_rb, heavy=True)
        return fm

    def build_back(ch, y_last_ref):
        xbuf = ch % 2
        x1 = x1_sb[xbuf]
        yT = xh_sb[xbuf]
        t0 = ch * TC
        bm = {}

        def t_x2():
            def _(v):
                v.wait_ge(sems["ss"], max(y_last_ref["y"]))
                for tt in range(NTT):
                    ins = v.tensor_add(x1[:, tt * DH:(tt + 1) * DH],
                                       x1[:, tt * DH:(tt + 1) * DH],
                                       yT[:, tt * DH:(tt + 1) * DH])
                ins.then_inc(sems["sv"], 1)
                bm["x2"] = bump("sv")
                trk["xh_rd_v"][xbuf] = bm["x2"]
            on("vector", _)
        defer(t_x2)

        if debug:
            def t_dbg2():
                def _(g, t0=t0):
                    g.wait_ge(sems["sv"], bm["x2"])
                    for tt in range(NTT):
                        dma(g, dbg["dbg_x2"][t0 + tt * 128:
                                             t0 + (tt + 1) * 128, :],
                            x1[:, tt * DH:(tt + 1) * DH], "gd")
                on("gpsimd", _)
            defer(t_dbg2)

        lz3 = {}

        def t_depfix3():
            lz3["sv"] = bm["x2"]
        defer(t_depfix3)
        ln3 = ln_phase(ch, "s3", xbuf, xbuf, lz3, 2)
        ffn2 = ffn_phase(ch, xbuf, ln3, "zq2", 1)

        def store_out(tt, state, t0=t0, xbuf=xbuf):
            def _(v, tt=tt):
                v.wait_ge(sems["st"], state["mm"])
                v.tensor_add(x1[:, tt * DH:(tt + 1) * DH], ps_z[:, :],
                             x1[:, tt * DH:(tt + 1) * DH]).then_inc(
                                 sems["sv"], 1)
            on("vector", _)
            m = bump("sv")
            if tt == NTT - 1:
                trk["psz_rd"] = m

            def _(g, tt=tt, m=m, t0=t0):
                g.wait_ge(sems["sv"], m)
                dma(g, out[t0 + tt * 128:t0 + (tt + 1) * 128, :],
                    x1[:, tt * DH:(tt + 1) * DH], "gd")
                if tt == 0:
                    trk["x1_free"][xbuf] = [None] * NTT
                trk["x1_free"][xbuf][tt] = cnt["gd"]
            on("gpsimd", _)
            return m
        out2_phase(ffn2, store_out)
        return bm

    gstate = {"gi": 0, "scanms": {}}

    def emit_scan_chunk(ch, fm, drain_per_dtile=DRAIN_PER_DTILE):
        xbuf = ch % 2
        xh = xh_sb[xbuf]
        yT = xh
        mkds = [dict() for _ in range(NDT)]
        rm = fm["rm"]
        dAv = dAb[:, :].rearrange("p (k t) -> p k t", t=SEGL)
        dBxv = dBxb[:, :].rearrange("p (k t) -> p k t", t=SEGL)
        hbv = hb[:, :].rearrange("p (k t) -> p k t", t=SEGL)
        brv = b_rep[:, :].rearrange("p (n t) -> p n t", t=TC)
        crv = c_rep[:, :].rearrange("p (n t) -> p n t", t=TC)

        def emit_prep(i):
            mkd = mkds[i]
            pL = i % 2

            def _(te, i=i, mkd=mkd, pL=pL,
                  pe=trk["pstr"][pL], pa=trk["psa"]):
                if i == 0:
                    te.wait_ge(sems["sg"], rm["rep"])
                    te.wait_ge(sems["ss"], fm["ln2"]["nm"])
                te.wait_ge(sems["ss"], pa)
                te.matmul(out=ps_a[:, :],
                          lhsT=w_dtwt[:, i * 128:(i + 1) * 128],
                          rhs=dtin[:, :], start=True,
                          stop=True).then_inc(sems["st"], 1)
                mkd["dt"] = bump("st")
                te.wait_ge(sems[pe[0]], pe[1])
                for tt in range(NTT):
                    ins = te.transpose(out=ps_tr[:, pL * 512 + tt * 128:
                                               pL * 512 + (tt + 1) * 128],
                                       in_=xh[:, tt * DH + i * 128:
                                              tt * DH + (i + 1) * 128],
                                       identity=w_id16[:, :])
                ins.then_inc(sems["st"], 1)
                mkd["x2tt"] = bump("st")
                trk["xh_rd"][xbuf] = mkd["x2tt"]
            on("tensor", _)

            def _(s, i=i, mkd=mkd, pL=pL, lxr=trk["Lx2_rd"][pL]):
                s.wait_ge(sems["st"], mkd["dt"])
                s.wait_ge(sems["sv"], lxr)
                s.activation(out=tsg[:, :], in_=ps_a[:, :], func=AF.Exp,
                             bias=w_dtbn[:, i:i + 1]).then_inc(sems["ss"], 1)
                s.wait_ge(sems["ss"], bump("ss"))
                s.activation(out=L_d[:, pL * TC:(pL + 1) * TC], in_=tsg[:, :],
                             func=AF.Ln, bias=1.0).then_inc(sems["ss"], 1)
                mkd["L"] = bump("ss")
                trk["psa"] = mkd["L"]
                s.wait_ge(sems["st"], mkd["x2tt"])
                s.activation(out=x2T_d[:, pL * TC:(pL + 1) * TC],
                             in_=ps_tr[:, pL * 512:(pL + 1) * 512],
                             func=AF.Identity).then_inc(sems["ss"], 1)
                mkd["x2te"] = bump("ss")
                trk["pstr"][pL] = ("ss", mkd["x2te"])
            on("scalar", _)

        def emit_da(i):
            mkd = mkds[i]
            pL = i % 2
            gi0 = gstate["gi"]
            guards = [gstate["scanms"].get(gi0 + g - 4) for g in range(NGPT)]

            def _(s, i=i, mkd=mkd, pL=pL, gi0=gi0, guards=guards):
                for g in range(NGPT):
                    if guards[g] is not None:
                        s.wait_ge(sems["sv"], guards[g])
                    base = ((gi0 + g) % 4) * NGRP * SEGL
                    for j in range(NGRP):
                        n = g * NGRP + j
                        ins = s.activation(
                            out=dAb[:, base + j * SEGL + 1:
                                    base + j * SEGL + 1 + TC],
                            in_=L_d[:, pL * TC:(pL + 1) * TC],
                            func=AF.Exp,
                            scale=w_absA[:, i * NST + n:i * NST + n + 1])
                    ins.then_inc(sems["ss"], 1)
                    mkd[f"dAg{g}"] = bump("ss")
            on("scalar", _)

        def emit_dve_tile(i):
            mkd = mkds[i]
            pL = i % 2
            gi0 = gstate["gi"]

            mkd_prev = mkds[i - 1] if i >= 1 else None

            def _(v, i=i, mkd=mkd, pL=pL, gi0=gi0, rep=rm["rep"],
                  ytr=trk["ytot_rd"][pL], mkd_prev=mkd_prev):
                v.wait_ge(sems["ss"], mkd["x2te"])
                v.wait_ge(sems["sg"], rep)
                v.tensor_mul(u_d[:, :],
                             L_d[:, pL * TC:(pL + 1) * TC],
                             x2T_d[:, pL * TC:(pL + 1) * TC])
                for g in range(NGPT):
                    par = (gi0 + g) % 4
                    n0 = g * NGRP
                    c0 = i * NST + n0
                    v.tensor_copy(dBxv[:, :, 0:1],
                                  carry[:, c0:c0 + NGRP].unsqueeze(2))
                    v.tensor_tensor(
                        out=dBxv[:, :, 1:SEGL],
                        in0=u_d[:, :].unsqueeze(1).broadcast_to(
                            [128, NGRP, TC]),
                        in1=brv[:, n0:n0 + NGRP, :], op=ALU.mult)
                    v.wait_ge(sems["ss"], mkd[f"dAg{g}"])
                    v.tensor_tensor_scan(
                        out=hb[:, :],
                        data0=dAb[:, par * NGRP * SEGL:
                                  (par + 1) * NGRP * SEGL],
                        data1=dBxb[:, :], initial=zc[:, 0:1],
                        op0=ALU.mult, op1=ALU.add).then_inc(sems["sv"], 1)
                    gstate["scanms"][gi0 + g] = bump("sv")
                    v.wait_ge(sems["sv"], gstate["scanms"][gi0 + g])
                    v.tensor_copy(carry[:, c0:c0 + NGRP].unsqueeze(2),
                                  hbv[:, :, SEGL - 1:SEGL])
                    v.tensor_tensor(
                        out=dBxb[:, 0:NGRP * TC].rearrange(
                            "p (n t) -> p n t", t=TC),
                        in0=hbv[:, :, 1:SEGL],
                        in1=crv[:, n0:n0 + NGRP, :], op=ALU.mult)
                    v.tensor_add(dBxb[:, 0:2 * TC], dBxb[:, 0:2 * TC],
                                 dBxb[:, 2 * TC:4 * TC])
                    v.tensor_add(dBxb[:, 0:TC], dBxb[:, 0:TC],
                                 dBxb[:, TC:2 * TC])
                    if g == 0:
                        v.tensor_copy(yacc[:, :], dBxb[:, 0:TC])
                    else:
                        v.tensor_add(yacc[:, :], yacc[:, :], dBxb[:, 0:TC])
                v.wait_ge(sems["st"], ytr)
                v.scalar_tensor_tensor(out=ytot[:, pL * TC:(pL + 1) * TC],
                                       in0=x2T_d[:, pL * TC:(pL + 1) * TC],
                                       scalar=w_D[:, i:i + 1],
                                       in1=yacc[:, :],
                                       op0=ALU.mult,
                                       op1=ALU.add).then_inc(sems["sv"], 1)
                mkd["y"] = bump("sv")
                trk["Lx2_rd"][pL] = mkd["y"]
            on("vector", _)
            gstate["gi"] = gi0 + NGPT

        def emit_yback(i):
            mkd = mkds[i]
            pL = i % 2

            def _(te, mkd=mkd, pL=pL, pg=trk["psyt"]):
                te.wait_ge(sems["sv"], mkd["y"])
                te.wait_ge(sems[pg[0]], pg[1])
                for tt in range(NTT):
                    ins = te.transpose(
                        out=ps_yt[:, tt * 128:(tt + 1) * 128],
                        in_=ytot[:, pL * TC + tt * 128:
                                 pL * TC + (tt + 1) * 128],
                        identity=w_id16[:, :])
                ins.then_inc(sems["st"], 1)
                mkd["yt"] = bump("st")
                trk["ytot_rd"][pL] = mkd["yt"]
            on("tensor", _)

            def _(sc, i=i, mkd=mkd):
                sc.wait_ge(sems["st"], mkd["yt"])
                sc.activation(
                    out=yT[:, :].rearrange("p (tt c) -> p tt c", tt=NTT)
                    [:, :, i * 128:(i + 1) * 128],
                    in_=ps_yt[:, 0:TC].rearrange("p (tt c) -> p tt c", tt=NTT),
                    func=AF.Identity).then_inc(sems["ss"], 1)
                mkd["yte"] = bump("ss")
                trk["psyt"] = ("ss", mkd["yte"])
            on("scalar", _)

        if drain_per_dtile == 0:
            drain()
        else:
            drain(12, light_only=True)
        for i in range(NDT):
            emit_prep(i)
            emit_da(i)
            if i >= 1:
                emit_yback(i - 1)
            emit_dve_tile(i)
            drain(drain_per_dtile, light_only=True)
        emit_yback(NDT - 1)

        if debug and ch == 0:
            def _(g):
                g.wait_ge(sems["ss"], max(mkds[NDT - 2]["yte"],
                                          mkds[NDT - 1]["yte"]))
                dma(g, dbg["dbg_yt"], yT[:, :], "gd")
            on("gpsimd", _)

        y_last = [mkds[NDT - 2]["yte"], mkds[NDT - 1]["yte"]]
        return y_last, mkds[NDT - 1]["y"], mkds[NDT - 1]["dt"]

    fronts = [None] * (nch + 1)
    fronts[0] = build_front(0)
    drain()

    prev_y_last = None
    for ch in range(nch):
        fm = fronts[ch]
        if ch > 0:
            build_back(ch - 1, {"y": prev_y_last})
            if SERIAL_BACK:
                drain()
        if ch + 1 < nch:
            fronts[ch + 1] = build_front(ch + 1)
        y_last, last_v, last_dt = emit_scan_chunk(ch, fm)
        prev_y_last = y_last
        if ch + 1 < nch:
            fronts[ch + 1]["dt_guard"] = last_dt
            fronts[ch + 1]["brep_guard"] = last_v
        drain()

    build_back(nch - 1, {"y": prev_y_last})
    drain()

    def _(g):
        g.wait_ge(sems["gd"], cnt["gd"])
        g.wait_ge(sems["sd"], cnt["sd"])
    on("gpsimd", _)

    ctx.close()
    return nc


def _shard_inputs(inputs):
    xs = np.asarray(inputs["L0_attn_x"])
    A_log = np.asarray(inputs["L0_attn_A_log"])
    Dv = np.asarray(inputs["L0_attn_D"])
    xpw = np.asarray(inputs["L0_attn_x_proj_w"])
    dtw = np.asarray(inputs["L0_attn_dt_proj_w"])
    dtbv = np.asarray(inputs["L0_attn_dt_proj_b"])
    Aw = np.asarray(inputs["L0_ffn_A_w"])
    Bw = np.asarray(inputs["L0_ffn_B_w"])
    Wq = np.asarray(inputs["L0_ffn_Wq_w"])
    keys = np.asarray(inputs["L0_ffn_keys"])

    ident16 = np.eye(128, dtype=np.float16)
    ident32 = np.eye(128, dtype=np.float32)
    selm = np.zeros((64, 4 * 128), dtype=np.float16)
    for mm in range(4):
        for p in range(128):
            selm[(mm * 128 + p) // 8, mm * 128 + p] = 1.0

    maps = []
    for cid in range(8):
        k, j = cid // 2, cid % 2
        sl = slice(j * DH, (j + 1) * DH)
        maps.append(dict(
            x_in=np.ascontiguousarray(xs[k][:, sl]),
            awt=np.ascontiguousarray(Aw[:, sl].T),
            bwt=np.ascontiguousarray(Bw[sl, :].T),
            wqt=np.ascontiguousarray(Wq[:, sl].T),
            keyst=np.ascontiguousarray(keys.T),
            xpt=np.ascontiguousarray(xpw[:, sl].T),
            dtwt=np.ascontiguousarray(dtw[sl, :].T),
            dtb=np.ascontiguousarray(dtbv[sl].reshape(NDT, 128).T),
            alog=np.ascontiguousarray(
                A_log[sl].reshape(NDT, 128, NST).transpose(1, 0, 2)
                .reshape(128, NDT * NST)),
            dvec=np.ascontiguousarray(Dv[sl].reshape(NDT, 128).T),
            ident16=ident16, ident32=ident32, sel=selm,
        ))
    return maps


def kernel(**inputs):
    seq_len = 4096
    nc = build_program(seq_len)
    maps = _shard_inputs(inputs)
    res = run_bass_kernel_spmd(nc, maps, core_ids=list(range(8)))
    full = np.empty((4, seq_len, DMODEL), dtype=np.float32)
    for cid in range(8):
        k, j = cid // 2, cid % 2
        full[k][:, j * 2048:(j + 1) * 2048] = res.results[cid]["out"]
    return full
